# version 60
# baseline (speedup 1.0000x reference)
import numpy as np
from contextlib import ExitStack

import concourse.bass as bass
import concourse.mybir as mybir
from concourse.bass_utils import run_bass_kernel_spmd

F32 = mybir.dt.float32
BF16 = mybir.dt.bfloat16
AF = mybir.ActivationFunctionType
ALU = mybir.AluOpType
AX = mybir.AxisListType

EPS = 1e-5



class Ctx:

    ENGINES = ("sp", "act", "dve", "pe", "pool")

    def __init__(self, nc, es):
        self.nc = nc
        self.streams = {e: [] for e in self.ENGINES}
        self.last_write = {}
        self.readers = {}
        self.ecount = {e: 0 for e in self.ENGINES}
        self.esem = {e: es.enter_context(nc.semaphore(f"s_{e}")) for e in self.ENGINES}
        self.dma_pools = {}
        self.dma_rr = {}
        self.dma_uses = {}
        for q, eng, n in (("qsp", "sp", 8), ("qact", "act", 8), ("qpool", "pool", 12)):
            sems = [es.enter_context(nc.semaphore(f"{q}_{i}")) for i in range(n)]
            self.dma_pools[q] = (eng, sems)
            self.dma_rr[q] = 0
            self.dma_uses[q] = [0] * n

    def _dep_events(self, reads, writes):
        evs = []
        for r in reads:
            ev = self.last_write.get(r)
            if ev is not None:
                evs.append(ev)
        for w in writes:
            ev = self.last_write.get(w)
            if ev is not None:
                evs.append(ev)
            evs.extend(self.readers.get(w, ()))
        return evs

    def _record(self, reads, writes, ev):
        for r in reads:
            self.readers.setdefault(r, []).append(ev)
        for w in writes:
            self.last_write[w] = ev
            self.readers[w] = []

    def op(self, engine, emit, reads=(), writes=(), tag=""):
        deps = self._dep_events(reads, writes)
        self.ecount[engine] += 1
        myev = (self.esem[engine], self.ecount[engine], engine)
        self.streams[engine].append(
            {"deps": deps, "emit": emit, "inc": (self.esem[engine], 1), "tag": tag,
             "engine": engine})
        self._record(reads, writes, myev)
        return myev

    def dma(self, q, emit, reads=(), writes=(), tag=""):
        eng, sems = self.dma_pools[q]
        i = self.dma_rr[q]
        self.dma_rr[q] = (i + 1) % len(sems)
        sem = sems[i]
        deps = self._dep_events(reads, writes)
        if self.dma_uses[q][i] > 0:
            deps = deps + [(sem, 16 * self.dma_uses[q][i], None)]
        self.dma_uses[q][i] += 1
        myev = (sem, 16 * self.dma_uses[q][i], None)
        self.streams[eng].append(
            {"deps": deps, "emit": emit, "inc": (sem, 16), "tag": tag, "engine": eng})
        self._record(reads, writes, myev)
        return myev

    def barrier_all_dma(self):
        for q, (eng, sems) in self.dma_pools.items():
            for i, sem in enumerate(sems):
                uses = self.dma_uses[q][i]
                if uses:
                    self.streams[eng].append(
                        {"deps": [(sem, 16 * uses, None)], "emit": None,
                         "inc": None, "tag": f"fin_{q}_{i}", "engine": eng})

    def emit_stream(self, engine, eng_obj):
        watermark = {}
        for rec in self.streams[engine]:
            for (sem, val, src_eng) in rec["deps"]:
                if src_eng == "pe" and engine == "pe":
                    continue
                key = sem.name
                if watermark.get(key, -1) >= val:
                    continue
                eng_obj.wait_ge(sem, val)
                watermark[key] = val
            if rec["emit"] is not None:
                instr = rec["emit"](eng_obj)
                if rec["inc"] is not None:
                    sem, n = rec["inc"]
                    instr.then_inc(sem, n)




def build_program(D=2048, T=2048, S=2048, Tt=512, NE=32, R=16, AD=32, debug=False, dbg_t=0):
    assert D % 512 == 0 and S % 128 == 0 and T % Tt == 0 and Tt % 128 == 0
    KD = D // 128
    assert KD % 4 == 0
    KQ = KD // 4
    MS = S // 128
    C = Tt // 128
    NT = T // Tt
    ND = D // 512
    G = (NE * R) // 128
    NSUB = D // 512

    nc = bass.Bass("TRN2", target_bir_lowering=False)

    x_h = nc.dram_tensor("x", [T, D], F32, kind="ExternalInput")
    wp_h = nc.dram_tensor("wp", [MS * 4 * 128, 5 * KQ * 128], F32, kind="ExternalInput")
    wo_h = nc.dram_tensor("wo", [ND * 4 * 128, (MS // 4) * 512], F32, kind="ExternalInput")
    u_h = nc.dram_tensor("u", [128, KD * NE * R], F32, kind="ExternalInput")
    v_h = nc.dram_tensor("v", [128, G * D], F32, kind="ExternalInput")
    wr_h = nc.dram_tensor("wr", [128, KD * NE], F32, kind="ExternalInput")
    rmsw_h = nc.dram_tensor("rmsw", [128, KD], F32, kind="ExternalInput")
    wfb_h = nc.dram_tensor("wfb", [128, MS], F32, kind="ExternalInput")
    woutb_h = nc.dram_tensor("woutb", [1, D], F32, kind="ExternalInput")
    attrt_h = nc.dram_tensor("attrt", [AD, NE], F32, kind="ExternalInput")
    wat_h = nc.dram_tensor("wat", [AD, 1], F32, kind="ExternalInput")
    id_h = nc.dram_tensor("idm", [128, 128], F32, kind="ExternalInput")
    e_h = nc.dram_tensor("eoh", [NE, G * 128], F32, kind="ExternalInput")
    ones_h = nc.dram_tensor("onesv", [1, 128], F32, kind="ExternalInput")
    out_h = nc.dram_tensor("out", [T, D], F32, kind="ExternalOutput")
    if debug:
        dbg = {
            "dbg_xntm": nc.dram_tensor("dbg_xntm", [128, C * D], BF16, kind="ExternalOutput"),
            "dbg_xnfm": nc.dram_tensor("dbg_xnfm", [128, KD * Tt], BF16, kind="ExternalOutput"),
            "dbg_gated": nc.dram_tensor("dbg_gated", [128, MS * Tt], BF16, kind="ExternalOutput"),
            "dbg_x1": nc.dram_tensor("dbg_x1", [128, C * D], F32, kind="ExternalOutput"),
            "dbg_h2fm": nc.dram_tensor("dbg_h2fm", [128, KD * Tt], BF16, kind="ExternalOutput"),
            "dbg_g": nc.dram_tensor("dbg_g", [128, C * NE], F32, kind="ExternalOutput"),
            "dbg_f": nc.dram_tensor("dbg_f", [128, Tt], BF16, kind="ExternalOutput"),
            "dbg_z": nc.dram_tensor("dbg_z", [128, Tt], BF16, kind="ExternalOutput"),
            "dbg_h": nc.dram_tensor("dbg_h", [128, Tt], BF16, kind="ExternalOutput"),
            "dbg_og": nc.dram_tensor("dbg_og", [128, Tt], BF16, kind="ExternalOutput"),
            "dbg_sga": nc.dram_tensor("dbg_sga", [128, Tt], BF16, kind="ExternalOutput"),
            "dbg_ub": nc.dram_tensor("dbg_ub", [128, Tt], BF16, kind="ExternalOutput"),
            "dbg_ps0": nc.dram_tensor("dbg_ps0", [128, Tt], F32, kind="ExternalOutput"),
            "dbg_wblk": nc.dram_tensor("dbg_wblk", [128, 5 * KQ * 128 * 4], BF16, kind="ExternalOutput"),
        }
        dbg_ps0sb = E2 = None

    es = ExitStack()
    E = es.enter_context

    X = E(nc.sbuf_tensor("X", [128, C * D], F32))
    xn_tm = E(nc.sbuf_tensor("xn_tm", [128, C * D], BF16))
    xn_fm = E(nc.sbuf_tensor("xn_fm", [128, KD * Tt], BF16))
    gated = E(nc.sbuf_tensor("gated", [128, MS * Tt], BF16))
    wblk = [E(nc.sbuf_tensor(f"wblk{i}", [128, 5 * KQ * 128], BF16)) for i in range(4)]
    woblk = [E(nc.sbuf_tensor(f"woblk{i}", [128, (MS // 4) * 512], BF16)) for i in range(4)]
    u_sb = E(nc.sbuf_tensor("u_sb", [128, KD * NE * R], BF16))
    v_sb = E(nc.sbuf_tensor("v_sb", [128, G * D], BF16))
    wr_sb = E(nc.sbuf_tensor("wr_sb", [128, KD * NE], BF16))
    rmsw = E(nc.sbuf_tensor("rmsw_s", [128, KD], F32))
    wfb = E(nc.sbuf_tensor("wfb_s", [128, MS], F32))
    bout = E(nc.sbuf_tensor("bout", [128, D], F32))
    attrt = E(nc.sbuf_tensor("attrt_s", [AD, NE], F32))
    wat = E(nc.sbuf_tensor("wat_s", [AD, 1], F32))
    aw = E(nc.sbuf_tensor("aw_s", [NE, 1], F32))
    id32 = E(nc.sbuf_tensor("id32", [128, 128], F32))
    idbf = E(nc.sbuf_tensor("idbf", [128, 128], BF16))
    ebf = E(nc.sbuf_tensor("ebf", [NE, G * 128], BF16))
    onesbf = E(nc.sbuf_tensor("onesbf", [1, 128], BF16))
    ones32 = E(nc.sbuf_tensor("ones32", [1, 128], F32))
    fb = [E(nc.sbuf_tensor(f"fb{i}", [128, Tt], BF16)) for i in range(2)]
    zb = [E(nc.sbuf_tensor(f"zb{i}", [128, Tt], BF16)) for i in range(2)]
    hb = [E(nc.sbuf_tensor(f"hb{i}", [128, Tt], BF16)) for i in range(2)]
    ogb = [E(nc.sbuf_tensor(f"ogb{i}", [128, Tt], BF16)) for i in range(2)]
    sgz = [E(nc.sbuf_tensor(f"sgz{i}", [128, Tt], BF16)) for i in range(2)]
    sga = [E(nc.sbuf_tensor(f"sga{i}", [128, Tt], BF16)) for i in range(2)]
    S0 = E(nc.sbuf_tensor("S0", [128, MS], BF16))
    stats = E(nc.sbuf_tensor("stats", [128, 6 * NSUB * C], F32))
    sv = E(nc.sbuf_tensor("sv", [128, 16], F32))
    svc = E(nc.sbuf_tensor("svc", [128, 16 * C], F32))
    svca = E(nc.sbuf_tensor("svca", [128, 16 * C], F32))
    xpre = E(nc.sbuf_tensor("xpre", [128, 2 * D], F32))
    cst = E(nc.sbuf_tensor("cst", [128, 2], F32))
    ps0cp = E(nc.sbuf_tensor("ps0cp", [128, Tt], F32)) if debug else None
    sc_sb = E(nc.sbuf_tensor("sc_sb", [NE, Tt], F32))
    sct = E(nc.sbuf_tensor("sct", [128, C * NE], F32))
    top8 = E(nc.sbuf_tensor("top8", [128, 8 * C], F32))
    zap = E(nc.sbuf_tensor("zap", [128, NE * C], F32))
    msk = E(nc.sbuf_tensor("msk", [128, NE * C], F32))
    pexp = E(nc.sbuf_tensor("pexp", [128, NE * C], F32))
    gv = E(nc.sbuf_tensor("gv", [128, 16], F32))
    gtok = E(nc.sbuf_tensor("gtok", [128, C * NE], F32))
    gfm = E(nc.sbuf_tensor("gfm", [NE, Tt], F32))
    g2 = E(nc.sbuf_tensor("g2", [NE, Tt], BF16))
    gexp = E(nc.sbuf_tensor("gexp", [128, G * Tt], BF16))
    gu = E(nc.sbuf_tensor("gu", [128, G * Tt], BF16))

    ps = [E(nc.psum_tensor(f"ps{i}", [128, 512], F32)) for i in range(8)]

    C_ = Ctx(nc, es)

    def wp_dma(m, kq):
        slot = kq
        blk = m * 4 + kq
        C_.dma("qpool",
               lambda e, slot=slot, blk=blk: e.dma_start(
                   out=wblk[slot][:], in_=wp_h[blk * 128:(blk + 1) * 128, :]),
               writes=[f"wblk{slot}"], tag=f"wp{m}_{kq}")

    def wo_dma(n, kg):
        slot = kg
        blk = n * 4 + kg
        C_.dma("qpool",
               lambda e, slot=slot, blk=blk: e.dma_start(
                   out=woblk[slot][:], in_=wo_h[blk * 128:(blk + 1) * 128, :]),
               writes=[f"woblk{slot}"], tag=f"wo{n}_{kg}")


    def ld(q, dst, src, w, tag):
        C_.dma(q, lambda e, d=dst, s=src: e.dma_start(out=d, in_=s), writes=[w], tag=tag)

    C_.dma("qsp", lambda e: e.dma_start(out=xpre[0:1, 0:D], in_=woutb_h[:, :]),
           writes=["woutb", "xpre0"], tag="ld_woutb")
    ld("qsp", attrt[:], attrt_h[:, :], "attrt", "ld_attrt")
    ld("qsp", wat[:], wat_h[:, :], "wat", "ld_wat")
    ld("qsp", id32[:], id_h[:, :], "id32", "ld_id")
    ld("qsp", ones32[:], ones_h[:, :], "ones32", "ld_ones")
    ld("qsp", rmsw[:], rmsw_h[:, :], "rmsw", "ld_rmsw")
    ld("qsp", wfb[:], wfb_h[:, :], "wfb", "ld_wfb")
    for c in range(C):
        for q in range(NSUB):
            qq = "qsp" if (c * NSUB + q) % 2 == 0 else "qact"
            C_.dma(qq, lambda e, c=c, q=q: e.dma_start(
                out=X[:, c * D + q * 512: c * D + (q + 1) * 512],
                in_=x_h[c * 128:(c + 1) * 128, q * 512:(q + 1) * 512]),
                writes=[f"xq{c}_{q}"], tag=f"ld_x0_{c}_{q}")
    for kq_ in range(4):
        wp_dma(0, kq_)
    ld("qpool", idbf[:], id_h[:, :], "idbf", "ld_idbf")
    ld("qpool", wr_sb[:], wr_h[:, :], "wr_sb", "ld_wr")
    ld("qpool", ebf[:], e_h[:, :], "ebf", "ld_e")
    ld("qpool", onesbf[:], ones_h[:, :], "onesbf", "ld_onesbf")

    def ld_uv_piece(i, npieces=8):
        uw = KD * NE * R // (npieces // 2)
        if i < npieces // 2:
            C_.dma("qpool", lambda e, i=i: e.dma_start(
                out=u_sb[:, i * uw:(i + 1) * uw], in_=u_h[:, i * uw:(i + 1) * uw]),
                writes=["u_sb"], tag=f"ld_u{i}")
        else:
            j = i - npieces // 2
            vw = G * D // (npieces // 2)
            C_.dma("qpool", lambda e, j=j: e.dma_start(
                out=v_sb[:, j * vw:(j + 1) * vw], in_=v_h[:, j * vw:(j + 1) * vw]),
                writes=["v_sb"], tag=f"ld_v{j}")
    for nb in range(ND):
        C_.op("pe", lambda e, nb=nb: e.matmul(
            ps[nb % 2][:, 0:512], ones32[0:1, :], xpre[0:1, nb * 512:(nb + 1) * 512],
            start=True, stop=True),
            reads=["woutb", "xpre0", "ones32"], writes=[f"pb{nb % 2}"], tag=f"bb{nb}")
        C_.op("act", lambda e, nb=nb: e.activation(
            bout[:, nb * 512:(nb + 1) * 512], ps[nb % 2][:, 0:512], AF.Identity),
            reads=[f"pb{nb % 2}"], writes=["bout"], tag=f"bbe{nb}")
    C_.op("pe", lambda e: e.matmul(ps[7][0:NE, 0:1], attrt[:, :], wat[:, :],
                                   start=True, stop=True),
          reads=["attrt", "wat"], writes=["pb7"])
    C_.op("act", lambda e: e.activation(aw[:], ps[7][0:NE, 0:1], AF.Sigmoid),
          reads=["pb7"], writes=["aw"])
    C_.op("dve", lambda e: e.memset(S0[:], 0.0),
          writes=[f"S0_{m}" for m in range(MS)])
    C_.op("dve", lambda e: e.memset(cst[:, 0:1], EPS), writes=["cst"])
    C_.op("dve", lambda e: e.memset(cst[:, 1:2], 0.9), writes=["cst"])

    def dump(name, src_ap, regions):
        if debug:
            C_.dma("qact", lambda e, s=src_ap, d=dbg[name]: e.dma_start(
                out=d[:, :], in_=s), reads=list(regions), tag=f"dump_{name}")

    def dump_part(name, idx, src_ap, regions):
        if debug:
            w = 5 * KQ * 128
            C_.dma("qact", lambda e, s=src_ap, d=dbg[name], i=idx: e.dma_start(
                out=d[:, i * w:(i + 1) * w], in_=s), reads=list(regions),
                tag=f"dump_{name}_{idx}")

    svv = svc[:].rearrange("p (c j) -> p c j", j=16)

    def emit_mnvar(t, kind, c, w, rv):
        C_.op("dve", lambda e, w=w: e.tensor_scalar_mul(
            w[:, 0:1], w[:, 13:14], 1.0 / D),
            reads=rv, writes=rv, tag=f"{kind}mn{t}_{c}")
        C_.op("dve", lambda e, w=w: e.tensor_tensor(
            w[:, 2:3], w[:, 0:1], w[:, 0:1], ALU.mult),
            reads=rv, writes=rv, tag=f"{kind}m2{t}_{c}")
        C_.op("dve", lambda e, w=w: e.scalar_tensor_tensor(
            w[:, 1:2], w[:, 14:15], 1.0 / D, w[:, 2:3], ALU.mult, ALU.subtract),
            reads=rv, writes=rv, tag=f"{kind}vr{t}_{c}")

    def ln_stage(t, kind, chunks, defer_xres=None, extra_reads=None,
                 phase="all"):
        if not chunks:
            return
        cs = [c for c, _, _, _, _, _ in chunks]
        svt, pfx = (svca, "sva") if kind == "a" else (svc, "sv")
        rvall = [f"{pfx}{c}" for c in cs]
        for (c, src, srgn, seng, darea, aeng) in (chunks if phase in ("all", "stats")
                                                  else ()):
            w = svt[:, c * 16:(c + 1) * 16]
            rv = [f"{pfx}{c}"]
            if seng in ("dve", "pool"):
                st = stats[:, 6 * NSUB * c:6 * NSUB * (c + 1)]
                rs = [f"stats{c}"]
                xr = [srgn] + (extra_reads or [])
                for i in range(NSUB):
                    C_.op(seng, lambda e, i=i, src=src, st=st: e.bn_stats(
                        st[:, 6 * i:6 * i + 6], src[:, 512 * i:512 * (i + 1)]),
                        reads=xr, writes=rs, tag=f"{kind}bns{t}_{c}_{i}")
                C_.op(seng, lambda e, w=w, st=st: e.bn_aggr(w[:, 0:2], st[:]),
                      reads=rs, writes=rv, tag=f"{kind}bna{t}_{c}")
            else:
                dmp = xn_tm[:, c * D:(c + 1) * D]
                drg = [f"xn_tm{c}"]
                xr = [srgn] + (extra_reads or [])
                C_.op("act", lambda e, src=src, w=w, dmp=dmp: e.activation(
                    dmp, src, AF.Identity, accum_out=w[:, 13:14]),
                    reads=xr, writes=drg + rv, tag=f"{kind}sum{t}_{c}")
                C_.op("act", lambda e, src=src, w=w, dmp=dmp: e.activation(
                    dmp, src, AF.Square, accum_out=w[:, 14:15]),
                    reads=xr, writes=drg + rv, tag=f"{kind}sq{t}_{c}")
                if phase == "all":
                    emit_mnvar(t, kind, c, w, rv)
        for idx, (c, src, srgn, seng, darea, aeng) in (
                enumerate(chunks) if phase in ("all", "rest") else ()):
            w = svt[:, c * 16:(c + 1) * 16]
            rv = [f"{pfx}{c}"]
            srd = [srgn] + (extra_reads or [])
            if phase == "rest" and seng == "act":
                emit_mnvar(t, kind, c, w, rv)
            C_.op("dve", lambda e, w=w: e.tensor_scalar_add(
                w[:, 2:3], w[:, 1:2], EPS), reads=rv, writes=rv)
            if kind == "a":
                C_.op("dve", lambda e, w=w: e.tensor_scalar(
                    w[:, 3:4], w[:, 1:2], 1.0 + EPS, EPS * EPS, ALU.mult, ALU.add),
                    reads=rv, writes=rv)
                C_.op("act", lambda e, w=w: e.activation(w[:, 4:6], w[:, 2:4], AF.Ln),
                      reads=rv, writes=rv)
                C_.op("act", lambda e, w=w: e.activation(w[:, 6:8], w[:, 4:6], AF.Exp,
                                                         scale=-0.5),
                      reads=rv, writes=rv)
                C_.op("dve", lambda e, w=w: e.tensor_scalar_add(
                    w[:, 8:9], w[:, 6:7], 1.0), reads=rv, writes=rv)
                C_.op("dve", lambda e, w=w: e.tensor_scalar(
                    w[:, 9:11], w[:, 6:8], w[:, 0:1], -1.0, ALU.mult, ALU.mult),
                    reads=rv, writes=rv)
                if aeng == "act":
                    C_.op("act", lambda e, c=c, src=src, w=w: e.activation(
                        xn_tm[:, c * D:(c + 1) * D], src, AF.Identity,
                        bias=w[:, 10:11], scale=w[:, 7:8]),
                        reads=srd + rv, writes=[f"xn_tm{c}"], tag=f"xn{t}_{c}")

                    def _xres(c=c, src=src, srd=srd, w=w, rv=rv, t=t):
                        C_.op("dve", lambda e: e.tensor_scalar(
                            X[:, c * D:(c + 1) * D], src, w[:, 8:9], w[:, 9:10],
                            ALU.mult, ALU.add),
                            reads=srd + rv, writes=[f"X{c}"], tag=f"xres{t}_{c}")
                    if defer_xres is not None:
                        defer_xres.append(_xres)
                    else:
                        _xres()
                else:
                    C_.op("dve", lambda e, c=c, src=src, w=w: e.tensor_scalar(
                        xn_tm[:, c * D:(c + 1) * D], src, w[:, 7:8], w[:, 10:11],
                        ALU.mult, ALU.add),
                        reads=srd + rv, writes=[f"xn_tm{c}"], tag=f"xn{t}_{c}")

                    def _xres2(c=c, src=src, srd=srd, w=w, rv=rv, t=t):
                        C_.op("act", lambda e: e.activation(
                            X[:, c * D:(c + 1) * D], src, AF.Identity,
                            bias=w[:, 9:10], scale=w[:, 8:9]),
                            reads=srd + rv, writes=[f"X{c}"], tag=f"xres{t}_{c}")
                    if defer_xres is not None:
                        defer_xres.append(_xres2)
                    else:
                        _xres2()
            else:
                C_.op("act", lambda e, w=w: e.activation(w[:, 4:5], w[:, 2:3], AF.Ln),
                      reads=rv, writes=rv)
                C_.op("act", lambda e, w=w: e.activation(w[:, 6:7], w[:, 4:5], AF.Exp,
                                                         scale=-0.5),
                      reads=rv, writes=rv)
                C_.op("dve", lambda e, w=w: e.scalar_tensor_tensor(
                    w[:, 9:10], w[:, 0:1], -1.0, w[:, 6:7], ALU.mult, ALU.mult),
                    reads=rv, writes=rv)
                if aeng == "act":
                    C_.op("act", lambda e, c=c, src=src, w=w: e.activation(
                        xn_tm[:, c * D:(c + 1) * D], src, AF.Identity,
                        bias=w[:, 9:10], scale=w[:, 6:7]),
                        reads=srd + rv, writes=[f"xn_tm{c}"], tag=f"h2{t}_{c}")
                else:
                    C_.op("dve", lambda e, c=c, src=src, w=w: e.tensor_scalar(
                        xn_tm[:, c * D:(c + 1) * D], src, w[:, 6:7], w[:, 9:10],
                        ALU.mult, ALU.add),
                        reads=srd + rv, writes=[f"xn_tm{c}"], tag=f"h2{t}_{c}")

    xn_fm_v = xn_fm[:].rearrange("p (k tt) -> p k tt", k=KD)

    def bc_trb(t, c, base=4):

        def trb_emit(e, c=c, base=base):
            last = None
            for k in range(KD):
                last = e.matmul(
                    ps[base + k // 4][:, (k % 4) * 128:(k % 4 + 1) * 128],
                    xn_tm[:, c * D + k * 128: c * D + (k + 1) * 128],
                    idbf[:], start=True, stop=True)
            return last
        C_.op("pe", trb_emit,
              reads=[f"xn_tm{c}", "idbf"],
              writes=[f"pb{base + b}" for b in range((KD + 3) // 4)],
              tag=f"trx{t}_{c}")

    def bc_evacs(t, c, base=4, scale=False, eng_all=None):
        if scale:
            for k in range(KD):
                b = k // 4
                src = ps[base + b][:, (k % 4) * 128:(k % 4 + 1) * 128]
                dst = xn_fm[:, k * Tt + c * 128: k * Tt + (c + 1) * 128]
                if eng_all == "act" or (eng_all is None and b % 2 == 0):
                    C_.op("act", lambda e, src=src, dst=dst, k=k: e.activation(
                        dst, src, AF.Identity, scale=rmsw[:, k:k + 1]),
                        reads=[f"pb{base + b}", "rmsw"], writes=[f"xn_fm{k}"],
                        tag=f"evx{t}_{c}_{k}")
                else:
                    C_.op("dve", lambda e, src=src, dst=dst, k=k: e.tensor_scalar_mul(
                        dst, src, rmsw[:, k:k + 1]),
                        reads=[f"pb{base + b}", "rmsw"], writes=[f"xn_fm{k}"],
                        tag=f"evx{t}_{c}_{k}")
            return
        for b in range((KD + 3) // 4):
            if b % 2 == 0:
                C_.op("act", lambda e, c=c, b=b, base=base: e.activation(
                    xn_fm_v[:, 4 * b:4 * b + 4, c * 128:(c + 1) * 128],
                    ps[base + b][:, 0:512].rearrange("p (k tt) -> p k tt", k=4),
                    AF.Identity),
                    reads=[f"pb{base + b}"],
                    writes=[f"xn_fm{4 * b + j}" for j in range(4)],
                    tag=f"evx{t}_{c}_{b}")
            else:
                C_.op("dve", lambda e, c=c, b=b, base=base: e.tensor_copy(
                    xn_fm_v[:, 4 * b:4 * b + 4, c * 128:(c + 1) * 128],
                    ps[base + b][:, 0:512].rearrange("p (k tt) -> p k tt", k=4)),
                    reads=[f"pb{base + b}"],
                    writes=[f"xn_fm{4 * b + j}" for j in range(4)],
                    tag=f"evx{t}_{c}_{b}")

    def b_chunk(t, c, base=4, scale=False):
        bc_trb(t, c, base)
        bc_evacs(t, c, base, scale)

    deferred = []
    for t in range(NT):
        tb = t * Tt

        if t == 0:
            prev = None
            for c in range(C):
                for q in range(NSUB):
                    C_.op("dve", lambda e, c=c, q=q: e.bn_stats(
                        stats[:, 6 * (NSUB * c + q):6 * (NSUB * c + q) + 6],
                        X[:, c * D + q * 512: c * D + (q + 1) * 512]),
                        reads=[f"xq{c}_{q}"], writes=[f"stats{c}"],
                        tag=f"abns0_{c}_{q}")
                C_.op("dve", lambda e, c=c: e.bn_aggr(
                    svca[:, c * 16:c * 16 + 2],
                    stats[:, 6 * NSUB * c:6 * NSUB * (c + 1)]),
                    reads=[f"stats{c}"], writes=[f"sva{c}"], tag=f"abna0_{c}")
                ln_stage(t, "a",
                         [(c, X[:, c * D:(c + 1) * D], f"xq{c}_0", "dve", 0,
                           "act" if c % 2 == 0 else "dve")],
                         extra_reads=[f"xq{c}_{q}" for q in range(NSUB)] +
                         [f"X{c}"],
                         phase="rest")
                if prev is not None:
                    bc_evacs(t, prev, 4 * ((prev + 1) % 2), scale=True)
                bc_trb(t, c, 4 * ((c + 1) % 2))
                prev = c
            bc_evacs(t, prev, 4 * ((prev + 1) % 2), scale=True)
        for fxr in deferred:
            fxr()
        deferred.clear()

        if t == dbg_t:
            dump("dbg_xntm", xn_tm[:], [f"xn_tm{c}" for c in range(C)])


        if t == dbg_t:
            dump("dbg_xnfm", xn_fm[:], [f"xn_fm{k}" for k in range(KD)])

        for m in range(MS):
            par = m % 2
            for kq in range(4):
                for pr in range(5):

                    def proj_emit(e, pr=pr, kq=kq):
                        last = None
                        for kk in range(KQ):
                            k = kq * KQ + kk
                            last = e.matmul(
                                ps[pr][:, 0:Tt],
                                wblk[kq][:, (pr * KQ + kk) * 128:(pr * KQ + kk + 1) * 128],
                                xn_fm[:, k * Tt:(k + 1) * Tt],
                                start=(kq == 0 and kk == 0),
                                stop=(kq == 3 and kk == KQ - 1))
                        return last
                    C_.op("pe", proj_emit,
                          reads=[f"wblk{kq}"] +
                                [f"xn_fm{kq * KQ + kk}" for kk in range(KQ)],
                          writes=[f"pb{pr}"], tag=f"proj{t}_{m}_{pr}_{kq}")
            if m + 1 < MS:
                for kq in range(4):
                    wp_dma(m + 1, kq)
            if t == 0 and MS > 8 and 1 <= m <= 8:
                ld_uv_piece(m - 1)
            if t == 0 and MS <= 8 and m == min(1, MS - 1):
                for i_ in range(8):
                    ld_uv_piece(i_)
            for kg_ in range(4):
                if m == max(0, MS - 6 + kg_):
                    wo_dma(0, kg_)
            C_.op("act", lambda e, par=par: e.activation(
                sga[par][:], ps[0][:, 0:Tt], AF.Silu),
                reads=["pb0"], writes=[f"sga{par}"], tag=f"gasil{t}_{m}")
            C_.op("dve", lambda e, par=par: e.tensor_tensor(
                ogb[par][:], sga[par][:], ps[1][:, 0:Tt], ALU.mult),
                reads=[f"sga{par}", "pb1"], writes=[f"ogb{par}"], tag=f"og{t}_{m}")
            C_.op("act", lambda e, par=par, m=m: e.activation(
                fb[par][:], ps[2][:, 0:Tt], AF.Sigmoid, bias=wfb[:, m:m + 1]),
                reads=["pb2", "wfb"], writes=[f"fb{par}"], tag=f"fsig{t}_{m}")
            C_.op("act", lambda e, par=par: e.activation(
                sgz[par][:], ps[3][:, 0:Tt], AF.Sigmoid),
                reads=["pb3"], writes=[f"sgz{par}"], tag=f"zgsig{t}_{m}")
            C_.op("dve", lambda e, par=par: e.tensor_tensor(
                zb[par][:], ps[4][:, 0:Tt], sgz[par][:], ALU.mult),
                reads=["pb4", f"sgz{par}"], writes=[f"zb{par}"], tag=f"z{t}_{m}")
            C_.op("dve", lambda e, par=par, m=m: e.tensor_tensor_scan(
                hb[par][:], fb[par][:], zb[par][:], S0[:, m:m + 1], ALU.mult, ALU.add),
                reads=[f"fb{par}", f"zb{par}", f"S0_{m}"], writes=[f"hb{par}"],
                tag=f"scan{t}_{m}")
            C_.op("dve", lambda e, par=par, m=m: e.tensor_copy(
                S0[:, m:m + 1], hb[par][:, Tt - 1:Tt]),
                reads=[f"hb{par}"], writes=[f"S0_{m}"], tag=f"stcp{t}_{m}")
            C_.op("dve", lambda e, par=par, m=m: e.tensor_tensor(
                gated[:, m * Tt:(m + 1) * Tt], ogb[par][:], hb[par][:], ALU.mult),
                reads=[f"ogb{par}", f"hb{par}"], writes=[f"gated{m}"], tag=f"gated{t}_{m}")
            if t == dbg_t and m == 0:
                dump("dbg_f", fb[par][:], [f"fb{par}"])
                dump("dbg_z", zb[par][:], [f"zb{par}"])
                dump("dbg_h", hb[par][:], [f"hb{par}"])
                dump("dbg_og", ogb[par][:], [f"ogb{par}"])
                dump("dbg_sga", sga[par][:], [f"sga{par}"])
                dump("dbg_ub", sgz[par][:], [f"sgz{par}"])

        if t == dbg_t:
            dump("dbg_gated", gated[:], [f"gated{m}" for m in range(MS)])

        for c in range(C):
            C_.op("dve", lambda e, c=c: e.tensor_tensor(
                X[:, c * D:(c + 1) * D], X[:, c * D:(c + 1) * D], bout[:],
                ALU.add),
                reads=[f"X{c}", "bout"], writes=[f"X{c}"], tag=f"xb{t}_{c}")
        if t + 1 < NT:
            for j, cpre in enumerate((C - 2, C - 1)):
                nrow0 = (t + 1) * Tt + cpre * 128
                C_.dma("qsp", lambda e, j=j, nrow0=nrow0: e.dma_start(
                    out=xpre[:, j * D:(j + 1) * D], in_=x_h[nrow0:nrow0 + 128, :]),
                    writes=[f"xpre{j}"], tag=f"ldxp{t + 1}_{j}")
        achunks = None
        if t + 1 < NT:
            gf_a = gated[:].bitcast(F32)
            nblk2 = 2 * D // Tt
            srcs = {C - 2: (xpre[:, 0:D], ["xpre0"]),
                    C - 1: (xpre[:, D:2 * D], ["xpre1"])}
            for j in range(min(2, C - 2)):
                srcs[j] = (gf_a[:, j * D:(j + 1) * D],
                           [f"gated{m}" for m in range(j * nblk2, (j + 1) * nblk2)])
            order = [C - 2, C - 1] + [j for j in range(C - 2) if j in srcs]
            aeng_l = ["act", "act", "act", "dve"]
            achunks = [(c, srcs[c][0], srcs[c][1][0], aeng_l[i],
                        0, "act" if i % 2 == 0 else "dve")
                       for i, c in enumerate(order)]
            aextra = sum((srcs[c][1] for c in order), [])
        for n in range(ND):
            for kg in range(4):
                for mt in range(C):
                    bank = (n * C + mt) % 8

                    def wout_q(e, n=n, mt=mt, bank=bank, kg=kg):
                        last = None
                        for kk in range(MS // 4):
                            k = kg * (MS // 4) + kk
                            last = e.matmul(
                                ps[bank][:, 0:512],
                                gated[:, k * Tt + mt * 128: k * Tt + (mt + 1) * 128],
                                woblk[kg][:, kk * 512:(kk + 1) * 512],
                                start=(k == 0), stop=(k == MS - 1))
                        return last
                    C_.op("pe", wout_q,
                          reads=([f"gated{k}" for k in range(kg * (MS // 4),
                                                            (kg + 1) * (MS // 4))] +
                                 [f"woblk{kg}"]),
                          writes=[f"pb{bank}"], tag=f"wout{t}_{n}_{mt}_{kg}")
            for mt in range(C):
                bank = (n * C + mt) % 8
                C_.op("dve", lambda e, n=n, mt=mt, bank=bank: e.tensor_tensor(
                    X[:, mt * D + n * 512: mt * D + (n + 1) * 512],
                    X[:, mt * D + n * 512: mt * D + (n + 1) * 512],
                    ps[bank][:, 0:512], ALU.add),
                    reads=[f"pb{bank}", f"X{mt}"], writes=[f"X{mt}"],
                    tag=f"xay{t}_{n}_{mt}")
                C_.op("dve", lambda e, n=n, mt=mt: e.bn_stats(
                    stats[:, 6 * (NSUB * mt + n):6 * (NSUB * mt + n) + 6],
                    X[:, mt * D + n * 512: mt * D + (n + 1) * 512]),
                    reads=[f"X{mt}"], writes=[f"stats{mt}"],
                    tag=f"ebns{t}_{mt}_{n}")
            if n == min(1, ND - 1) and achunks is not None:
                ln_stage(t + 1, "a", achunks[0:2], extra_reads=aextra,
                         phase="stats")
                C_.op("act", lambda e: e.activation(sv[0:1, 1:2], cst[0:1, 1:2],
                                                    AF.Exp),
                      reads=["cst"], writes=["svx"], tag=f"tbl{t}")
            if n == ND - 1:
                ech = [(mt, X[:, mt * D:(mt + 1) * D], f"X{mt}",
                        "dve", 0,
                        "dve" if mt % 2 == 0 else "act") for mt in range(C)]
                for mt in range(C):
                    C_.op("dve", lambda e, mt=mt: e.bn_aggr(
                        svc[:, mt * 16:mt * 16 + 2],
                        stats[:, 6 * NSUB * mt:6 * NSUB * (mt + 1)]),
                        reads=[f"stats{mt}"], writes=[f"sv{mt}"],
                        tag=f"ebna{t}_{mt}")
                prev = None
                for mt in range(C):
                    ln_stage(t, "e", [ech[mt]], phase="rest")
                    if prev is not None:
                        bc_evacs(t, prev, 4 * ((prev + 1) % 2))
                    bc_trb(t, mt, 4 * ((mt + 1) % 2))
                    prev = mt
                bc_evacs(t, prev, 4 * ((prev + 1) % 2))
            if n + 1 < ND:
                for kg in range(4):
                    wo_dma(n + 1, kg)

        if t + 1 < NT:
            for kq in range(4):
                wp_dma(0, kq)

        if t + 1 < NT:
            gf = gated[:].bitcast(F32)
            nblk2 = 2 * D // Tt
            for j in range(min(2, C - 2)):
                nrow0 = (t + 1) * Tt + j * 128
                C_.dma("qsp", lambda e, j=j, nrow0=nrow0, gf=gf: e.dma_start(
                    out=gf[:, j * D:(j + 1) * D], in_=x_h[nrow0:nrow0 + 128, :]),
                    writes=[f"gated{m}" for m in range(j * nblk2, (j + 1) * nblk2)],
                    tag=f"ldxg{t + 1}_{j}")

        if t == dbg_t:
            dump("dbg_x1", X[:], [f"X{mt}" for mt in range(C)])

        if t == dbg_t:
            dump("dbg_h2fm", xn_fm[:], [f"xn_fm{k}" for k in range(KD)])

        def sc_emit(e):
            last = None
            for k in range(KD):
                last = e.matmul(ps[0][0:NE, 0:Tt], wr_sb[:, k * NE:(k + 1) * NE],
                                xn_fm[:, k * Tt:(k + 1) * Tt],
                                start=(k == 0), stop=(k == KD - 1))
            return last
        C_.op("pe", sc_emit, reads=["wr_sb"] + [f"xn_fm{k}" for k in range(KD)],
              writes=["pb0"], tag=f"sc{t}")
        C_.op("act", lambda e: e.activation(sc_sb[:], ps[0][0:NE, 0:Tt], AF.Identity),
              reads=["pb0"], writes=["sc_sb"], tag=f"scev{t}")

        def sct_emit(e):
            last = None
            for c in range(C):
                last = e.matmul(ps[1][:, c * NE:(c + 1) * NE],
                                sc_sb[:, c * 128:(c + 1) * 128], id32[0:NE, 0:NE],
                                start=True, stop=True)
            return last
        C_.op("pe", sct_emit, reads=["sc_sb", "id32"], writes=["pb1"], tag=f"sct{t}")
        C_.op("act", lambda e: e.activation(sct[:], ps[1][:, 0:C * NE], AF.Identity),
              reads=["pb1"], writes=["sct"], tag=f"sctev{t}")

        for g in range(G):
            bank = 4 + g

            def hu_emit(e, g=g, bank=bank):
                last = None
                for k in range(KD):
                    last = e.matmul(
                        ps[bank][:, 0:Tt],
                        u_sb[:, k * NE * R + g * 128: k * NE * R + (g + 1) * 128],
                        xn_fm[:, k * Tt:(k + 1) * Tt],
                        start=(k == 0), stop=(k == KD - 1))
                return last
            C_.op("pe", hu_emit, reads=["u_sb"] + [f"xn_fm{k}" for k in range(KD)],
                  writes=[f"pb{bank}"], tag=f"hu{t}_{g}")

        sct3 = sct[:].rearrange("p (c e) -> p c e", e=NE)
        top3 = top8[:].rearrange("p (c j) -> p c j", j=8)
        pexp3 = pexp[:].rearrange("p (c e) -> p c e", e=NE)
        maxv = top3[:, :, 0:1].rearrange("p c j -> p (c j)")
        for c in range(C):
            C_.op("dve", lambda e, c=c: e.max(out=top8[:, c * 8:(c + 1) * 8],
                                              in_=sct[:, c * NE:(c + 1) * NE]),
                  reads=["sct"], writes=["top8"], tag=f"max{t}_{c}")
        C_.op("dve", lambda e: e.memset(top3[:, :, 2:8], -1e30),
              reads=[], writes=["top8"], tag=f"m8{t}")
        for c in range(C):
            C_.op("dve", lambda e, c=c: e.match_replace(
                out=zap[:, c * NE:(c + 1) * NE], in_to_replace=top8[:, c * 8:c * 8 + 8],
                in_values=sct[:, c * NE:(c + 1) * NE], imm_value=-1e30),
                reads=["sct", "top8"], writes=["zap"], tag=f"mr{t}_{c}")
        C_.op("dve", lambda e: e.tensor_tensor(msk[:], zap[:], sct[:], ALU.not_equal),
              reads=["zap", "sct"], writes=["msk"], tag=f"msk{t}")
        C_.op("dve", lambda e: e.tensor_scalar_mul(gv[:, 0:C], maxv, -1.0),
              reads=["top8"], writes=["gv"], tag=f"negm{t}")
        C_.op("dve", lambda e: e.tensor_tensor(
            pexp3, sct3, gv[:, 0:C].rearrange("p (c o) -> p c o", o=1).to_broadcast(
                [128, C, NE]), ALU.add),
            reads=["sct", "gv"], writes=["pexp"], tag=f"sub{t}")
        C_.op("act", lambda e: e.activation(pexp[:], pexp[:], AF.Exp),
              reads=["pexp"], writes=["pexp"], tag=f"exp{t}")
        C_.op("dve", lambda e: e.tensor_reduce(gv[:, 4:4 + C], pexp3, AX.X, ALU.add),
              reads=["pexp"], writes=["gv"], tag=f"den{t}")
        C_.op("dve", lambda e: e.reciprocal(gv[:, 8:8 + C], gv[:, 4:4 + C]),
              reads=["gv"], writes=["gv"], tag=f"rden{t}")
        C_.op("dve", lambda e: e.tensor_tensor(
            pexp3, pexp3, gv[:, 8:8 + C].rearrange("p (c o) -> p c o", o=1).to_broadcast(
                [128, C, NE]), ALU.mult),
            reads=["pexp", "gv"], writes=["pexp"], tag=f"phat{t}")
        C_.op("dve", lambda e: e.tensor_tensor(pexp[:], pexp[:], msk[:], ALU.mult),
              reads=["pexp", "msk"], writes=["pexp"], tag=f"g0{t}")
        C_.op("dve", lambda e: e.tensor_reduce(gv[:, 4:4 + C], pexp3, AX.X, ALU.add),
              reads=["pexp"], writes=["gv"], tag=f"gs{t}")
        C_.op("dve", lambda e: e.tensor_scalar_add(gv[:, 4:4 + C], gv[:, 4:4 + C], 1e-8),
              reads=["gv"], writes=["gv"], tag=f"gse{t}")
        C_.op("dve", lambda e: e.reciprocal(gv[:, 8:8 + C], gv[:, 4:4 + C]),
              reads=["gv"], writes=["gv"], tag=f"rgs{t}")
        C_.op("dve", lambda e: e.tensor_tensor(
            gtok[:].rearrange("p (c e) -> p c e", e=NE), pexp3,
            gv[:, 8:8 + C].rearrange("p (c o) -> p c o", o=1).to_broadcast([128, C, NE]),
            ALU.mult),
            reads=["pexp", "gv"], writes=["gtok"], tag=f"g{t}")

        def a_bn_part(ch, subs, do_aggr):
            (c_, src_, srgn_, _se, _da, _ae) = ch
            for i in subs:
                C_.op("dve", lambda e, i=i, src_=src_, c_=c_: e.bn_stats(
                    stats[:, 6 * (NSUB * c_ + i):6 * (NSUB * c_ + i) + 6],
                    src_[:, 512 * i:512 * (i + 1)]),
                    reads=[srgn_] + aextra, writes=[f"stats{c_}"],
                    tag=f"abns{t + 1}_{c_}_{i}")
            if do_aggr:
                C_.op("dve", lambda e, c_=c_: e.bn_aggr(
                    svca[:, c_ * 16:c_ * 16 + 2],
                    stats[:, 6 * NSUB * c_:6 * NSUB * (c_ + 1)]),
                    reads=[f"stats{c_}"], writes=[f"sva{c_}"],
                    tag=f"abna{t + 1}_{c_}")

        if achunks is not None and len(achunks) > 3:
            a_bn_part(achunks[3], range(0, (NSUB + 1) // 2), False)

        if t == dbg_t:
            dump("dbg_g", gtok[:], ["gtok"])

        def gt_emit(e):
            last = None
            for c in range(C):
                last = e.matmul(ps[2][0:NE, c * 128:(c + 1) * 128],
                                gtok[:, c * NE:(c + 1) * NE], id32[:, :],
                                start=True, stop=True)
            return last
        C_.op("pe", gt_emit, reads=["gtok", "id32"], writes=["pb2"], tag=f"gt{t}")
        C_.op("act", lambda e: e.activation(gfm[:], ps[2][0:NE, 0:Tt], AF.Identity),
              reads=["pb2"], writes=["gfm"], tag=f"gfmev{t}")
        C_.op("pe", lambda e: e.matmul(ps[3][0:1, 0:Tt], aw[:], gfm[:],
                                       start=True, stop=True),
              reads=["aw", "gfm"], writes=["pb3"], tag=f"gate{t}")
        C_.op("act", lambda e: e.activation(sc_sb[0:1, 0:Tt], ps[3][0:1, 0:Tt], AF.Identity,
                                            bias=cst[0:1, 1:2], scale=0.2),
              reads=["pb3", "cst"], writes=["sc_sb"], tag=f"fac{t}")
        C_.op("pe", lambda e: e.matmul(ps[3][0:NE, 0:Tt], ones32[:, 0:NE], sc_sb[0:1, 0:Tt],
                                       start=True, stop=True),
              reads=["ones32", "sc_sb"], writes=["pb3"], tag=f"facx{t}")
        C_.op("dve", lambda e: e.tensor_tensor(g2[:], gfm[:], ps[3][0:NE, 0:Tt], ALU.mult),
              reads=["gfm", "pb3"], writes=["g2"], tag=f"g2{t}")
        if achunks is not None and len(achunks) > 3:
            a_bn_part(achunks[3], range((NSUB + 1) // 2, NSUB), True)
        for g in range(G):
            bank = g % 2
            C_.op("pe", lambda e, g=g, bank=bank: e.matmul(
                ps[bank][:, 0:Tt], ebf[:, g * 128:(g + 1) * 128],
                g2[:], start=True, stop=True),
                reads=["ebf", "g2"], writes=[f"pb{bank}"], tag=f"gexmm{t}_{g}")
            C_.op("act", lambda e, g=g, bank=bank: e.activation(
                gexp[:, g * Tt:(g + 1) * Tt], ps[bank][:, 0:Tt], AF.Identity),
                reads=[f"pb{bank}"], writes=[f"gexp{g}"], tag=f"gexev{t}_{g}")
        for g in range(G):
            bank = 4 + g
            C_.op("dve", lambda e, g=g, bank=bank: e.tensor_tensor(
                gu[:, g * Tt:(g + 1) * Tt], ps[bank][:, 0:Tt],
                gexp[:, g * Tt:(g + 1) * Tt], ALU.mult),
                reads=[f"pb{bank}", f"gexp{g}"], writes=[f"gu{g}"], tag=f"gu{t}_{g}")
        for mt in range(C):
            for n in range(ND):
                bank = 4 + ((mt * ND + n) % 4)

                def v_emit(e, mt=mt, n=n, bank=bank):
                    last = None
                    for k2 in range(G):
                        last = e.matmul(
                            ps[bank][:, 0:512],
                            gu[:, k2 * Tt + mt * 128: k2 * Tt + (mt + 1) * 128],
                            v_sb[:, k2 * D + n * 512: k2 * D + (n + 1) * 512],
                            start=(k2 == 0), stop=(k2 == G - 1))
                    return last
                C_.op("pe", v_emit, reads=[f"gu{k2}" for k2 in range(G)] + ["v_sb"],
                      writes=[f"pb{bank}"], tag=f"vmm{t}_{mt}_{n}")
                C_.op("dve", lambda e, mt=mt, n=n, bank=bank: e.tensor_tensor(
                    X[:, mt * D + n * 512: mt * D + (n + 1) * 512],
                    X[:, mt * D + n * 512: mt * D + (n + 1) * 512],
                    ps[bank][:, 0:512], ALU.add),
                    reads=[f"pb{bank}", f"X{mt}"], writes=[f"X{mt}"],
                    tag=f"xaf{t}_{mt}_{n}")
            row0 = tb + mt * 128
            C_.dma("qsp", lambda e, mt=mt, row0=row0: e.dma_start(
                out=out_h[row0:row0 + 128, :], in_=X[:, mt * D:(mt + 1) * D]),
                reads=[f"X{mt}"], tag=f"st{t}_{mt}")
            if achunks is not None:
                tail_order = [cc for (cc, *_rest) in achunks]
                if mt == 0:
                    ln_stage(t + 1, "a", achunks[0:2], defer_xres=deferred,
                             extra_reads=aextra, phase="rest")
                    ln_stage(t + 1, "a", achunks[2:3], extra_reads=aextra,
                             phase="stats")
                if mt == min(1, C - 2):
                    ln_stage(t + 1, "a", achunks[2:3], defer_xres=deferred,
                             extra_reads=aextra, phase="rest")
                    ln_stage(t + 1, "a", achunks[3:4], defer_xres=deferred,
                             extra_reads=aextra, phase="rest")
                j = mt - 1
                if 0 <= j < len(tail_order) - 1:
                    bc_trb(t + 1, tail_order[j], base=0)
                    bc_evacs(t + 1, tail_order[j], base=0, scale=True)

        if t + 1 < NT:
            last = [cc for (cc, *_rest) in achunks][-1]
            bc_trb(t + 1, last, base=4)
            bc_evacs(t + 1, last, base=4, scale=True)

    C_.barrier_all_dma()

    with es, nc.Block() as block:
        @block.sync
        def _(e):
            C_.emit_stream("sp", e)

        @block.scalar
        def _(e):
            C_.emit_stream("act", e)

        @block.vector
        def _(e):
            C_.emit_stream("dve", e)

        @block.tensor
        def _(e):
            C_.emit_stream("pe", e)

        @block.gpsimd
        def _(e):
            C_.emit_stream("pool", e)

    return nc




def host_layouts(ins, D, S, NE=32, R=16, AD=32):
    MS = S // 128
    KD = D // 128
    KQ = KD // 4
    ND = D // 512
    G = NE * R // 128
    order = ("Wga_w", "Wgb_w", "Wf_w", "Wzg_w", "Wz_w")
    wp6 = np.empty((MS, 4, 128, 5, KQ, 128), dtype=np.float32)
    for pr, name in enumerate(order):
        W = np.asarray(ins[f"L0_attn_{name}"], dtype=np.float32)
        T4 = W.reshape(MS, 128, KD, 128).transpose(0, 3, 2, 1)
        T6 = T4.reshape(MS, 128, 4, KQ, 128).transpose(0, 2, 1, 3, 4)
        wp6[:, :, :, pr] = T6
    wp = np.ascontiguousarray(wp6).reshape(MS * 4 * 128, 5 * KQ * 128)
    WO = np.asarray(ins["L0_attn_Wout_w"], dtype=np.float32)
    wo4 = WO.reshape(ND, 512, MS, 128).transpose(0, 2, 3, 1)
    wo = np.ascontiguousarray(
        wo4.reshape(ND, 4, MS // 4, 128, 512).transpose(0, 1, 3, 2, 4)
    ).reshape(ND * 4 * 128, (MS // 4) * 512)
    U = np.asarray(ins["L0_ffn_U"], dtype=np.float32)
    u = np.ascontiguousarray(
        U.transpose(1, 0, 2).reshape(KD, 128, NE * R).transpose(1, 0, 2)
    ).reshape(128, KD * NE * R)
    V = np.asarray(ins["L0_ffn_V"], dtype=np.float32)
    v = np.ascontiguousarray(
        V.reshape(G, 128, D).transpose(1, 0, 2)).reshape(128, G * D)
    Wr = np.asarray(ins["L0_ffn_Wr"], dtype=np.float32)
    wr = np.ascontiguousarray(
        Wr.T.reshape(KD, 128, NE).transpose(1, 0, 2)).reshape(128, KD * NE)
    rmsw = np.ascontiguousarray(
        np.asarray(ins["L0_attn_rms_w"], dtype=np.float32).reshape(KD, 128).T)
    wfb = np.ascontiguousarray(
        np.asarray(ins["L0_attn_Wf_b"], dtype=np.float32).reshape(MS, 128).T)
    woutb = np.ascontiguousarray(
        np.asarray(ins["L0_attn_Wout_b"], dtype=np.float32).reshape(1, D))
    attrt = np.ascontiguousarray(np.asarray(ins["L0_ffn_attr"], dtype=np.float32).T)
    wat = np.ascontiguousarray(
        np.asarray(ins["L0_ffn_Wa"], dtype=np.float32).reshape(1, AD).T)
    idm = np.eye(128, dtype=np.float32)
    eoh = np.zeros((NE, G * 128), dtype=np.float32)
    for g in range(G):
        for j in range(128):
            eoh[g * (128 // R) + j // R, g * 128 + j] = 1.0
    onesv = np.ones((1, 128), dtype=np.float32)
    return dict(wp=wp, wo=wo, u=u, v=v, wr=wr, rmsw=rmsw, wfb=wfb, woutb=woutb,
                attrt=attrt, wat=wat, idm=idm, eoh=eoh, onesv=onesv)


def kernel(**ins):
    x = np.asarray(ins["L0_attn_x"], dtype=np.float32)
    B, T, D = x.shape
    shared = host_layouts(ins, D=D, S=D)
    nc = build_program(D=D, T=T, S=D, Tt=512)
    in_maps = []
    for b in range(B):
        m = {"x": np.ascontiguousarray(x[b])}
        m.update(shared)
        in_maps.append(m)
    res = run_bass_kernel_spmd(nc, in_maps, core_ids=list(range(B)))
    out = np.stack([np.asarray(res.results[b]["out"]) for b in range(B)], axis=0)
    return out.astype(np.float32)


# version 61
# speedup vs baseline: 1.1913x; 1.1913x over previous
import numpy as np
from contextlib import ExitStack

import concourse.bass as bass
import concourse.mybir as mybir
from concourse.bass_utils import run_bass_kernel_spmd

F32 = mybir.dt.float32
BF16 = mybir.dt.bfloat16
AF = mybir.ActivationFunctionType
ALU = mybir.AluOpType
AX = mybir.AxisListType

EPS = 1e-5



class Ctx:

    ENGINES = ("sp", "act", "dve", "pe", "pool")

    def __init__(self, nc, es):
        self.nc = nc
        self.streams = {e: [] for e in self.ENGINES}
        self.last_write = {}
        self.readers = {}
        self.ecount = {e: 0 for e in self.ENGINES}
        self.esem = {e: es.enter_context(nc.semaphore(f"s_{e}")) for e in self.ENGINES}
        self.dma_pools = {}
        self.dma_rr = {}
        self.dma_uses = {}
        for q, eng, n in (("qsp", "sp", 8), ("qact", "act", 8), ("qpool", "pool", 12)):
            sems = [es.enter_context(nc.semaphore(f"{q}_{i}")) for i in range(n)]
            self.dma_pools[q] = (eng, sems)
            self.dma_rr[q] = 0
            self.dma_uses[q] = [0] * n

    def _dep_events(self, reads, writes):
        evs = []
        for r in reads:
            ev = self.last_write.get(r)
            if ev is not None:
                evs.append(ev)
        for w in writes:
            ev = self.last_write.get(w)
            if ev is not None:
                evs.append(ev)
            evs.extend(self.readers.get(w, ()))
        return evs

    def _record(self, reads, writes, ev):
        for r in reads:
            self.readers.setdefault(r, []).append(ev)
        for w in writes:
            self.last_write[w] = ev
            self.readers[w] = []

    def op(self, engine, emit, reads=(), writes=(), tag=""):
        deps = self._dep_events(reads, writes)
        self.ecount[engine] += 1
        myev = (self.esem[engine], self.ecount[engine], engine)
        self.streams[engine].append(
            {"deps": deps, "emit": emit, "inc": (self.esem[engine], 1), "tag": tag,
             "engine": engine})
        self._record(reads, writes, myev)
        return myev

    def dma(self, q, emit, reads=(), writes=(), tag=""):
        eng, sems = self.dma_pools[q]
        i = self.dma_rr[q]
        self.dma_rr[q] = (i + 1) % len(sems)
        sem = sems[i]
        deps = self._dep_events(reads, writes)
        if self.dma_uses[q][i] > 0:
            deps = deps + [(sem, 16 * self.dma_uses[q][i], None)]
        self.dma_uses[q][i] += 1
        myev = (sem, 16 * self.dma_uses[q][i], None)
        self.streams[eng].append(
            {"deps": deps, "emit": emit, "inc": (sem, 16), "tag": tag, "engine": eng})
        self._record(reads, writes, myev)
        return myev

    def barrier_all_dma(self):
        for q, (eng, sems) in self.dma_pools.items():
            for i, sem in enumerate(sems):
                uses = self.dma_uses[q][i]
                if uses:
                    self.streams[eng].append(
                        {"deps": [(sem, 16 * uses, None)], "emit": None,
                         "inc": None, "tag": f"fin_{q}_{i}", "engine": eng})

    def emit_stream(self, engine, eng_obj):
        watermark = {}
        for rec in self.streams[engine]:
            for (sem, val, src_eng) in rec["deps"]:
                if src_eng == "pe" and engine == "pe":
                    continue
                key = sem.name
                if watermark.get(key, -1) >= val:
                    continue
                eng_obj.wait_ge(sem, val)
                watermark[key] = val
            if rec["emit"] is not None:
                instr = rec["emit"](eng_obj)
                if rec["inc"] is not None:
                    sem, n = rec["inc"]
                    instr.then_inc(sem, n)




def build_program(D=2048, T=2048, S=2048, Tt=512, NE=32, R=16, AD=32, debug=False, dbg_t=0):
    assert D % 512 == 0 and S % 128 == 0 and T % Tt == 0 and Tt % 128 == 0
    KD = D // 128
    assert KD % 4 == 0
    KQ = KD // 4
    MS = S // 128
    C = Tt // 128
    NT = T // Tt
    ND = D // 512
    G = (NE * R) // 128
    NSUB = D // 512

    nc = bass.Bass("TRN2", target_bir_lowering=False)

    x_h = nc.dram_tensor("x", [T, D], F32, kind="ExternalInput")
    wp_h = nc.dram_tensor("wp", [MS * 4 * 128, 5 * KQ * 128], F32, kind="ExternalInput")
    wo_h = nc.dram_tensor("wo", [ND * 4 * 128, (MS // 4) * 512], F32, kind="ExternalInput")
    u_h = nc.dram_tensor("u", [128, KD * NE * R], F32, kind="ExternalInput")
    v_h = nc.dram_tensor("v", [128, G * D], F32, kind="ExternalInput")
    wr_h = nc.dram_tensor("wr", [128, KD * NE], F32, kind="ExternalInput")
    rmsw_h = nc.dram_tensor("rmsw", [128, KD], F32, kind="ExternalInput")
    wfb_h = nc.dram_tensor("wfb", [128, MS], F32, kind="ExternalInput")
    woutb_h = nc.dram_tensor("woutb", [1, D], F32, kind="ExternalInput")
    attrt_h = nc.dram_tensor("attrt", [AD, NE], F32, kind="ExternalInput")
    wat_h = nc.dram_tensor("wat", [AD, 1], F32, kind="ExternalInput")
    id_h = nc.dram_tensor("idm", [128, 128], F32, kind="ExternalInput")
    e_h = nc.dram_tensor("eoh", [NE, G * 128], F32, kind="ExternalInput")
    ones_h = nc.dram_tensor("onesv", [1, 128], F32, kind="ExternalInput")
    out_h = nc.dram_tensor("out", [T, D], F32, kind="ExternalOutput")
    if debug:
        dbg = {
            "dbg_xntm": nc.dram_tensor("dbg_xntm", [128, C * D], BF16, kind="ExternalOutput"),
            "dbg_xnfm": nc.dram_tensor("dbg_xnfm", [128, KD * Tt], BF16, kind="ExternalOutput"),
            "dbg_gated": nc.dram_tensor("dbg_gated", [128, MS * Tt], BF16, kind="ExternalOutput"),
            "dbg_x1": nc.dram_tensor("dbg_x1", [128, C * D], F32, kind="ExternalOutput"),
            "dbg_h2fm": nc.dram_tensor("dbg_h2fm", [128, KD * Tt], BF16, kind="ExternalOutput"),
            "dbg_g": nc.dram_tensor("dbg_g", [128, C * NE], F32, kind="ExternalOutput"),
            "dbg_f": nc.dram_tensor("dbg_f", [128, Tt], BF16, kind="ExternalOutput"),
            "dbg_z": nc.dram_tensor("dbg_z", [128, Tt], BF16, kind="ExternalOutput"),
            "dbg_h": nc.dram_tensor("dbg_h", [128, Tt], BF16, kind="ExternalOutput"),
            "dbg_og": nc.dram_tensor("dbg_og", [128, Tt], BF16, kind="ExternalOutput"),
            "dbg_sga": nc.dram_tensor("dbg_sga", [128, Tt], BF16, kind="ExternalOutput"),
            "dbg_ub": nc.dram_tensor("dbg_ub", [128, Tt], BF16, kind="ExternalOutput"),
            "dbg_ps0": nc.dram_tensor("dbg_ps0", [128, Tt], F32, kind="ExternalOutput"),
            "dbg_wblk": nc.dram_tensor("dbg_wblk", [128, 5 * KQ * 128 * 4], BF16, kind="ExternalOutput"),
        }
        dbg_ps0sb = E2 = None

    es = ExitStack()
    E = es.enter_context

    X = E(nc.sbuf_tensor("X", [128, C * D], F32))
    xn_tm = E(nc.sbuf_tensor("xn_tm", [128, C * D], BF16))
    xn_fm = E(nc.sbuf_tensor("xn_fm", [128, KD * Tt], BF16))
    gated = E(nc.sbuf_tensor("gated", [128, MS * Tt], BF16))
    wblk = [E(nc.sbuf_tensor(f"wblk{i}", [128, 5 * KQ * 128], BF16)) for i in range(4)]
    woblk = [E(nc.sbuf_tensor(f"woblk{i}", [128, (MS // 4) * 512], BF16)) for i in range(4)]
    u_sb = E(nc.sbuf_tensor("u_sb", [128, KD * NE * R], BF16))
    v_sb = E(nc.sbuf_tensor("v_sb", [128, G * D], BF16))
    wr_sb = E(nc.sbuf_tensor("wr_sb", [128, KD * NE], BF16))
    rmsw = E(nc.sbuf_tensor("rmsw_s", [128, KD], F32))
    wfb = E(nc.sbuf_tensor("wfb_s", [128, MS], F32))
    bout = E(nc.sbuf_tensor("bout", [128, D], F32))
    attrt = E(nc.sbuf_tensor("attrt_s", [AD, NE], F32))
    wat = E(nc.sbuf_tensor("wat_s", [AD, 1], F32))
    aw = E(nc.sbuf_tensor("aw_s", [NE, 1], F32))
    id32 = E(nc.sbuf_tensor("id32", [128, 128], F32))
    idbf = E(nc.sbuf_tensor("idbf", [128, 128], BF16))
    ebf = E(nc.sbuf_tensor("ebf", [NE, G * 128], BF16))
    onesbf = E(nc.sbuf_tensor("onesbf", [1, 128], BF16))
    ones32 = E(nc.sbuf_tensor("ones32", [1, 128], F32))
    fb = [E(nc.sbuf_tensor(f"fb{i}", [128, Tt], BF16)) for i in range(2)]
    zb = [E(nc.sbuf_tensor(f"zb{i}", [128, Tt], BF16)) for i in range(2)]
    hb = [E(nc.sbuf_tensor(f"hb{i}", [128, Tt], BF16)) for i in range(2)]
    ogb = [E(nc.sbuf_tensor(f"ogb{i}", [128, Tt], BF16)) for i in range(2)]
    sgz = [E(nc.sbuf_tensor(f"sgz{i}", [128, Tt], BF16)) for i in range(2)]
    sga = [E(nc.sbuf_tensor(f"sga{i}", [128, Tt], BF16)) for i in range(2)]
    S0 = E(nc.sbuf_tensor("S0", [128, MS], BF16))
    stats = E(nc.sbuf_tensor("stats", [128, 6 * NSUB * C], F32))
    sv = E(nc.sbuf_tensor("sv", [128, 16], F32))
    svc = E(nc.sbuf_tensor("svc", [128, 16 * C], F32))
    svca = E(nc.sbuf_tensor("svca", [128, 16 * C], F32))
    xpre = E(nc.sbuf_tensor("xpre", [128, 2 * D], F32))
    cst = E(nc.sbuf_tensor("cst", [128, 2], F32))
    ps0cp = E(nc.sbuf_tensor("ps0cp", [128, Tt], F32)) if debug else None
    sc_sb = E(nc.sbuf_tensor("sc_sb", [NE, Tt], F32))
    sct = E(nc.sbuf_tensor("sct", [128, C * NE], F32))
    top8 = E(nc.sbuf_tensor("top8", [128, 8 * C], F32))
    zap = E(nc.sbuf_tensor("zap", [128, NE * C], F32))
    msk = E(nc.sbuf_tensor("msk", [128, NE * C], F32))
    pexp = E(nc.sbuf_tensor("pexp", [128, NE * C], F32))
    gv = E(nc.sbuf_tensor("gv", [128, 16], F32))
    gtok = E(nc.sbuf_tensor("gtok", [128, C * NE], F32))
    gfm = E(nc.sbuf_tensor("gfm", [NE, Tt], F32))
    g2 = E(nc.sbuf_tensor("g2", [NE, Tt], BF16))
    gexp = E(nc.sbuf_tensor("gexp", [128, G * Tt], BF16))
    gu = E(nc.sbuf_tensor("gu", [128, G * Tt], BF16))

    ps = [E(nc.psum_tensor(f"ps{i}", [128, 512], F32)) for i in range(8)]

    C_ = Ctx(nc, es)

    def wp_dma(m, kq):
        slot = kq
        blk = m * 4 + kq
        C_.dma("qpool",
               lambda e, slot=slot, blk=blk: e.dma_start(
                   out=wblk[slot][:], in_=wp_h[blk * 128:(blk + 1) * 128, :]),
               writes=[f"wblk{slot}"], tag=f"wp{m}_{kq}")

    def wo_dma(n, kg):
        slot = kg
        blk = n * 4 + kg
        C_.dma("qpool",
               lambda e, slot=slot, blk=blk: e.dma_start(
                   out=woblk[slot][:], in_=wo_h[blk * 128:(blk + 1) * 128, :]),
               writes=[f"woblk{slot}"], tag=f"wo{n}_{kg}")


    def ld(q, dst, src, w, tag):
        C_.dma(q, lambda e, d=dst, s=src: e.dma_start(out=d, in_=s), writes=[w], tag=tag)

    C_.dma("qsp", lambda e: e.dma_start(out=xpre[0:1, 0:D], in_=woutb_h[:, :]),
           writes=["woutb", "xpre0"], tag="ld_woutb")
    ld("qsp", attrt[:], attrt_h[:, :], "attrt", "ld_attrt")
    ld("qsp", wat[:], wat_h[:, :], "wat", "ld_wat")
    ld("qsp", id32[:], id_h[:, :], "id32", "ld_id")
    ld("qsp", ones32[:], ones_h[:, :], "ones32", "ld_ones")
    ld("qsp", rmsw[:], rmsw_h[:, :], "rmsw", "ld_rmsw")
    ld("qsp", wfb[:], wfb_h[:, :], "wfb", "ld_wfb")
    def ld_xq(qq, c, q):
        C_.dma(qq, lambda e, c=c, q=q: e.dma_start(
            out=X[:, c * D + q * 512: c * D + (q + 1) * 512],
            in_=x_h[c * 128:(c + 1) * 128, q * 512:(q + 1) * 512]),
            writes=[f"xq{c}_{q}"], tag=f"ld_x0_{c}_{q}")

    qpool_xq = [(C - 1, q) for q in range(NSUB)] +                ([(C - 2, NSUB - 1)] if C > 1 and NSUB > 1 else [])
    i_ = 0
    for c in range(C):
        for q in range(NSUB):
            if (c, q) in qpool_xq:
                continue
            ld_xq("qsp" if i_ % 2 == 0 else "qact", c, q)
            i_ += 1
    for kq_ in range(4):
        wp_dma(0, kq_)
        if kq_ < len(qpool_xq):
            ld_xq("qpool", *qpool_xq[kq_])
    for cq_ in qpool_xq[4:]:
        ld_xq("qpool", *cq_)
    ld("qpool", idbf[:], id_h[:, :], "idbf", "ld_idbf")
    ld("qpool", wr_sb[:], wr_h[:, :], "wr_sb", "ld_wr")
    ld("qpool", ebf[:], e_h[:, :], "ebf", "ld_e")
    ld("qpool", onesbf[:], ones_h[:, :], "onesbf", "ld_onesbf")

    def ld_uv_piece(i, npieces=8):
        uw = KD * NE * R // (npieces // 2)
        if i < npieces // 2:
            C_.dma("qpool", lambda e, i=i: e.dma_start(
                out=u_sb[:, i * uw:(i + 1) * uw], in_=u_h[:, i * uw:(i + 1) * uw]),
                writes=["u_sb"], tag=f"ld_u{i}")
        else:
            j = i - npieces // 2
            vw = G * D // (npieces // 2)
            C_.dma("qpool", lambda e, j=j: e.dma_start(
                out=v_sb[:, j * vw:(j + 1) * vw], in_=v_h[:, j * vw:(j + 1) * vw]),
                writes=["v_sb"], tag=f"ld_v{j}")
    for nb in range(ND):
        C_.op("pe", lambda e, nb=nb: e.matmul(
            ps[nb % 2][:, 0:512], ones32[0:1, :], xpre[0:1, nb * 512:(nb + 1) * 512],
            start=True, stop=True),
            reads=["woutb", "xpre0", "ones32"], writes=[f"pb{nb % 2}"], tag=f"bb{nb}")
        C_.op("act", lambda e, nb=nb: e.activation(
            bout[:, nb * 512:(nb + 1) * 512], ps[nb % 2][:, 0:512], AF.Identity),
            reads=[f"pb{nb % 2}"], writes=["bout"], tag=f"bbe{nb}")
    C_.op("pe", lambda e: e.matmul(ps[7][0:NE, 0:1], attrt[:, :], wat[:, :],
                                   start=True, stop=True),
          reads=["attrt", "wat"], writes=["pb7"])
    C_.op("act", lambda e: e.activation(aw[:], ps[7][0:NE, 0:1], AF.Sigmoid),
          reads=["pb7"], writes=["aw"])
    C_.op("dve", lambda e: e.memset(S0[:], 0.0),
          writes=[f"S0_{m}" for m in range(MS)])
    C_.op("dve", lambda e: e.memset(cst[:, 0:1], EPS), writes=["cst"])
    C_.op("dve", lambda e: e.memset(cst[:, 1:2], 0.9), writes=["cst"])

    def dump(name, src_ap, regions):
        if debug:
            C_.dma("qact", lambda e, s=src_ap, d=dbg[name]: e.dma_start(
                out=d[:, :], in_=s), reads=list(regions), tag=f"dump_{name}")

    def dump_part(name, idx, src_ap, regions):
        if debug:
            w = 5 * KQ * 128
            C_.dma("qact", lambda e, s=src_ap, d=dbg[name], i=idx: e.dma_start(
                out=d[:, i * w:(i + 1) * w], in_=s), reads=list(regions),
                tag=f"dump_{name}_{idx}")

    svv = svc[:].rearrange("p (c j) -> p c j", j=16)

    def emit_mnvar(t, kind, c, w, rv):
        C_.op("dve", lambda e, w=w: e.tensor_scalar_mul(
            w[:, 0:1], w[:, 13:14], 1.0 / D),
            reads=rv, writes=rv, tag=f"{kind}mn{t}_{c}")
        C_.op("dve", lambda e, w=w: e.tensor_tensor(
            w[:, 2:3], w[:, 0:1], w[:, 0:1], ALU.mult),
            reads=rv, writes=rv, tag=f"{kind}m2{t}_{c}")
        C_.op("dve", lambda e, w=w: e.scalar_tensor_tensor(
            w[:, 1:2], w[:, 14:15], 1.0 / D, w[:, 2:3], ALU.mult, ALU.subtract),
            reads=rv, writes=rv, tag=f"{kind}vr{t}_{c}")

    def ln_stage(t, kind, chunks, defer_xres=None, extra_reads=None,
                 phase="all"):
        if not chunks:
            return
        cs = [c for c, _, _, _, _, _ in chunks]
        svt, pfx = (svca, "sva") if kind == "a" else (svc, "sv")
        rvall = [f"{pfx}{c}" for c in cs]
        for (c, src, srgn, seng, darea, aeng) in (chunks if phase in ("all", "stats")
                                                  else ()):
            w = svt[:, c * 16:(c + 1) * 16]
            rv = [f"{pfx}{c}"]
            if seng in ("dve", "pool"):
                st = stats[:, 6 * NSUB * c:6 * NSUB * (c + 1)]
                rs = [f"stats{c}"]
                xr = [srgn] + (extra_reads or [])
                for i in range(NSUB):
                    C_.op(seng, lambda e, i=i, src=src, st=st: e.bn_stats(
                        st[:, 6 * i:6 * i + 6], src[:, 512 * i:512 * (i + 1)]),
                        reads=xr, writes=rs, tag=f"{kind}bns{t}_{c}_{i}")
                C_.op(seng, lambda e, w=w, st=st: e.bn_aggr(w[:, 0:2], st[:]),
                      reads=rs, writes=rv, tag=f"{kind}bna{t}_{c}")
            else:
                dmp = xn_tm[:, c * D:(c + 1) * D]
                drg = [f"xn_tm{c}"]
                xr = [srgn] + (extra_reads or [])
                C_.op("act", lambda e, src=src, w=w, dmp=dmp: e.activation(
                    dmp, src, AF.Identity, accum_out=w[:, 13:14]),
                    reads=xr, writes=drg + rv, tag=f"{kind}sum{t}_{c}")
                C_.op("act", lambda e, src=src, w=w, dmp=dmp: e.activation(
                    dmp, src, AF.Square, accum_out=w[:, 14:15]),
                    reads=xr, writes=drg + rv, tag=f"{kind}sq{t}_{c}")
                if phase == "all":
                    emit_mnvar(t, kind, c, w, rv)
        for idx, (c, src, srgn, seng, darea, aeng) in (
                enumerate(chunks) if phase in ("all", "rest") else ()):
            w = svt[:, c * 16:(c + 1) * 16]
            rv = [f"{pfx}{c}"]
            srd = [srgn] + (extra_reads or [])
            if phase == "rest" and seng == "act":
                emit_mnvar(t, kind, c, w, rv)
            C_.op("dve", lambda e, w=w: e.tensor_scalar_add(
                w[:, 2:3], w[:, 1:2], EPS), reads=rv, writes=rv)
            if kind == "a":
                C_.op("dve", lambda e, w=w: e.tensor_scalar(
                    w[:, 3:4], w[:, 1:2], 1.0 + EPS, EPS * EPS, ALU.mult, ALU.add),
                    reads=rv, writes=rv)
                C_.op("act", lambda e, w=w: e.activation(w[:, 4:6], w[:, 2:4], AF.Ln),
                      reads=rv, writes=rv)
                C_.op("act", lambda e, w=w: e.activation(w[:, 6:8], w[:, 4:6], AF.Exp,
                                                         scale=-0.5),
                      reads=rv, writes=rv)
                C_.op("dve", lambda e, w=w: e.tensor_scalar_add(
                    w[:, 8:9], w[:, 6:7], 1.0), reads=rv, writes=rv)
                C_.op("dve", lambda e, w=w: e.tensor_scalar(
                    w[:, 9:11], w[:, 6:8], w[:, 0:1], -1.0, ALU.mult, ALU.mult),
                    reads=rv, writes=rv)
                if aeng == "act":
                    C_.op("act", lambda e, c=c, src=src, w=w: e.activation(
                        xn_tm[:, c * D:(c + 1) * D], src, AF.Identity,
                        bias=w[:, 10:11], scale=w[:, 7:8]),
                        reads=srd + rv, writes=[f"xn_tm{c}"], tag=f"xn{t}_{c}")

                    def _xres(c=c, src=src, srd=srd, w=w, rv=rv, t=t):
                        C_.op("dve", lambda e: e.tensor_scalar(
                            X[:, c * D:(c + 1) * D], src, w[:, 8:9], w[:, 9:10],
                            ALU.mult, ALU.add),
                            reads=srd + rv, writes=[f"X{c}"], tag=f"xres{t}_{c}")
                    if defer_xres is not None:
                        defer_xres.append(_xres)
                    else:
                        _xres()
                else:
                    C_.op("dve", lambda e, c=c, src=src, w=w: e.tensor_scalar(
                        xn_tm[:, c * D:(c + 1) * D], src, w[:, 7:8], w[:, 10:11],
                        ALU.mult, ALU.add),
                        reads=srd + rv, writes=[f"xn_tm{c}"], tag=f"xn{t}_{c}")

                    def _xres2(c=c, src=src, srd=srd, w=w, rv=rv, t=t):
                        C_.op("act", lambda e: e.activation(
                            X[:, c * D:(c + 1) * D], src, AF.Identity,
                            bias=w[:, 9:10], scale=w[:, 8:9]),
                            reads=srd + rv, writes=[f"X{c}"], tag=f"xres{t}_{c}")
                    if defer_xres is not None:
                        defer_xres.append(_xres2)
                    else:
                        _xres2()
            else:
                C_.op("act", lambda e, w=w: e.activation(w[:, 4:5], w[:, 2:3], AF.Ln),
                      reads=rv, writes=rv)
                C_.op("act", lambda e, w=w: e.activation(w[:, 6:7], w[:, 4:5], AF.Exp,
                                                         scale=-0.5),
                      reads=rv, writes=rv)
                C_.op("dve", lambda e, w=w: e.scalar_tensor_tensor(
                    w[:, 9:10], w[:, 0:1], -1.0, w[:, 6:7], ALU.mult, ALU.mult),
                    reads=rv, writes=rv)
                if aeng == "act":
                    C_.op("act", lambda e, c=c, src=src, w=w: e.activation(
                        xn_tm[:, c * D:(c + 1) * D], src, AF.Identity,
                        bias=w[:, 9:10], scale=w[:, 6:7]),
                        reads=srd + rv, writes=[f"xn_tm{c}"], tag=f"h2{t}_{c}")
                else:
                    C_.op("dve", lambda e, c=c, src=src, w=w: e.tensor_scalar(
                        xn_tm[:, c * D:(c + 1) * D], src, w[:, 6:7], w[:, 9:10],
                        ALU.mult, ALU.add),
                        reads=srd + rv, writes=[f"xn_tm{c}"], tag=f"h2{t}_{c}")

    xn_fm_v = xn_fm[:].rearrange("p (k tt) -> p k tt", k=KD)

    def bc_trb(t, c, base=4):

        def trb_emit(e, c=c, base=base):
            last = None
            for k in range(KD):
                last = e.matmul(
                    ps[base + k // 4][:, (k % 4) * 128:(k % 4 + 1) * 128],
                    xn_tm[:, c * D + k * 128: c * D + (k + 1) * 128],
                    idbf[:], start=True, stop=True)
            return last
        C_.op("pe", trb_emit,
              reads=[f"xn_tm{c}", "idbf"],
              writes=[f"pb{base + b}" for b in range((KD + 3) // 4)],
              tag=f"trx{t}_{c}")

    def bc_evacs(t, c, base=4, scale=False, eng_all=None):
        if scale:
            for k in range(KD):
                b = k // 4
                src = ps[base + b][:, (k % 4) * 128:(k % 4 + 1) * 128]
                dst = xn_fm[:, k * Tt + c * 128: k * Tt + (c + 1) * 128]
                if eng_all == "act" or (eng_all is None and b % 2 == 0):
                    C_.op("act", lambda e, src=src, dst=dst, k=k: e.activation(
                        dst, src, AF.Identity, scale=rmsw[:, k:k + 1]),
                        reads=[f"pb{base + b}", "rmsw"], writes=[f"xn_fm{k}"],
                        tag=f"evx{t}_{c}_{k}")
                else:
                    C_.op("dve", lambda e, src=src, dst=dst, k=k: e.tensor_scalar_mul(
                        dst, src, rmsw[:, k:k + 1]),
                        reads=[f"pb{base + b}", "rmsw"], writes=[f"xn_fm{k}"],
                        tag=f"evx{t}_{c}_{k}")
            return
        for b in range((KD + 3) // 4):
            if b % 2 == 0:
                C_.op("act", lambda e, c=c, b=b, base=base: e.activation(
                    xn_fm_v[:, 4 * b:4 * b + 4, c * 128:(c + 1) * 128],
                    ps[base + b][:, 0:512].rearrange("p (k tt) -> p k tt", k=4),
                    AF.Identity),
                    reads=[f"pb{base + b}"],
                    writes=[f"xn_fm{4 * b + j}" for j in range(4)],
                    tag=f"evx{t}_{c}_{b}")
            else:
                C_.op("dve", lambda e, c=c, b=b, base=base: e.tensor_copy(
                    xn_fm_v[:, 4 * b:4 * b + 4, c * 128:(c + 1) * 128],
                    ps[base + b][:, 0:512].rearrange("p (k tt) -> p k tt", k=4)),
                    reads=[f"pb{base + b}"],
                    writes=[f"xn_fm{4 * b + j}" for j in range(4)],
                    tag=f"evx{t}_{c}_{b}")

    def b_chunk(t, c, base=4, scale=False):
        bc_trb(t, c, base)
        bc_evacs(t, c, base, scale)

    deferred = []
    for t in range(NT):
        tb = t * Tt

        if t == 0:
            prev = None
            for c in range(C):
                for q in range(NSUB):
                    C_.op("dve", lambda e, c=c, q=q: e.bn_stats(
                        stats[:, 6 * (NSUB * c + q):6 * (NSUB * c + q) + 6],
                        X[:, c * D + q * 512: c * D + (q + 1) * 512]),
                        reads=[f"xq{c}_{q}"], writes=[f"stats{c}"],
                        tag=f"abns0_{c}_{q}")
                C_.op("dve", lambda e, c=c: e.bn_aggr(
                    svca[:, c * 16:c * 16 + 2],
                    stats[:, 6 * NSUB * c:6 * NSUB * (c + 1)]),
                    reads=[f"stats{c}"], writes=[f"sva{c}"], tag=f"abna0_{c}")
                ln_stage(t, "a",
                         [(c, X[:, c * D:(c + 1) * D], f"xq{c}_0", "dve", 0,
                           "act" if c % 2 == 0 else "dve")],
                         extra_reads=[f"xq{c}_{q}" for q in range(NSUB)] +
                         [f"X{c}"],
                         phase="rest")
                if prev is not None:
                    bc_evacs(t, prev, 4 * ((prev + 1) % 2), scale=True)
                bc_trb(t, c, 4 * ((c + 1) % 2))
                prev = c
            bc_evacs(t, prev, 4 * ((prev + 1) % 2), scale=True)
        for fxr in deferred:
            fxr()
        deferred.clear()

        if t == dbg_t:
            dump("dbg_xntm", xn_tm[:], [f"xn_tm{c}" for c in range(C)])


        if t == dbg_t:
            dump("dbg_xnfm", xn_fm[:], [f"xn_fm{k}" for k in range(KD)])

        for m in range(MS):
            par = m % 2
            for kq in range(4):
                for pr in range(5):

                    def proj_emit(e, pr=pr, kq=kq):
                        last = None
                        for kk in range(KQ):
                            k = kq * KQ + kk
                            last = e.matmul(
                                ps[pr][:, 0:Tt],
                                wblk[kq][:, (pr * KQ + kk) * 128:(pr * KQ + kk + 1) * 128],
                                xn_fm[:, k * Tt:(k + 1) * Tt],
                                start=(kq == 0 and kk == 0),
                                stop=(kq == 3 and kk == KQ - 1))
                        return last
                    C_.op("pe", proj_emit,
                          reads=[f"wblk{kq}"] +
                                [f"xn_fm{kq * KQ + kk}" for kk in range(KQ)],
                          writes=[f"pb{pr}"], tag=f"proj{t}_{m}_{pr}_{kq}")
            if m + 1 < MS:
                for kq in range(4):
                    wp_dma(m + 1, kq)
            if t == 0 and MS > 8 and 1 <= m <= 8:
                ld_uv_piece(m - 1)
            if t == 0 and MS <= 8 and m == min(1, MS - 1):
                for i_ in range(8):
                    ld_uv_piece(i_)
            for kg_ in range(4):
                if m == max(0, MS - 6 + kg_):
                    wo_dma(0, kg_)
            C_.op("act", lambda e, par=par: e.activation(
                sga[par][:], ps[0][:, 0:Tt], AF.Silu),
                reads=["pb0"], writes=[f"sga{par}"], tag=f"gasil{t}_{m}")
            C_.op("dve", lambda e, par=par: e.tensor_tensor(
                ogb[par][:], sga[par][:], ps[1][:, 0:Tt], ALU.mult),
                reads=[f"sga{par}", "pb1"], writes=[f"ogb{par}"], tag=f"og{t}_{m}")
            C_.op("act", lambda e, par=par, m=m: e.activation(
                fb[par][:], ps[2][:, 0:Tt], AF.Sigmoid, bias=wfb[:, m:m + 1]),
                reads=["pb2", "wfb"], writes=[f"fb{par}"], tag=f"fsig{t}_{m}")
            C_.op("act", lambda e, par=par: e.activation(
                sgz[par][:], ps[3][:, 0:Tt], AF.Sigmoid),
                reads=["pb3"], writes=[f"sgz{par}"], tag=f"zgsig{t}_{m}")
            C_.op("dve", lambda e, par=par: e.tensor_tensor(
                zb[par][:], ps[4][:, 0:Tt], sgz[par][:], ALU.mult),
                reads=["pb4", f"sgz{par}"], writes=[f"zb{par}"], tag=f"z{t}_{m}")
            C_.op("dve", lambda e, par=par, m=m: e.tensor_tensor_scan(
                hb[par][:], fb[par][:], zb[par][:], S0[:, m:m + 1], ALU.mult, ALU.add),
                reads=[f"fb{par}", f"zb{par}", f"S0_{m}"], writes=[f"hb{par}"],
                tag=f"scan{t}_{m}")
            C_.op("dve", lambda e, par=par, m=m: e.tensor_copy(
                S0[:, m:m + 1], hb[par][:, Tt - 1:Tt]),
                reads=[f"hb{par}"], writes=[f"S0_{m}"], tag=f"stcp{t}_{m}")
            C_.op("dve", lambda e, par=par, m=m: e.tensor_tensor(
                gated[:, m * Tt:(m + 1) * Tt], ogb[par][:], hb[par][:], ALU.mult),
                reads=[f"ogb{par}", f"hb{par}"], writes=[f"gated{m}"], tag=f"gated{t}_{m}")
            if t == dbg_t and m == 0:
                dump("dbg_f", fb[par][:], [f"fb{par}"])
                dump("dbg_z", zb[par][:], [f"zb{par}"])
                dump("dbg_h", hb[par][:], [f"hb{par}"])
                dump("dbg_og", ogb[par][:], [f"ogb{par}"])
                dump("dbg_sga", sga[par][:], [f"sga{par}"])
                dump("dbg_ub", sgz[par][:], [f"sgz{par}"])

        if t == dbg_t:
            dump("dbg_gated", gated[:], [f"gated{m}" for m in range(MS)])

        for c in range(C):
            C_.op("dve", lambda e, c=c: e.tensor_tensor(
                X[:, c * D:(c + 1) * D], X[:, c * D:(c + 1) * D], bout[:],
                ALU.add),
                reads=[f"X{c}", "bout"], writes=[f"X{c}"], tag=f"xb{t}_{c}")
        if t + 1 < NT:
            for j, cpre in enumerate((C - 2, C - 1)):
                nrow0 = (t + 1) * Tt + cpre * 128
                C_.dma("qsp", lambda e, j=j, nrow0=nrow0: e.dma_start(
                    out=xpre[:, j * D:(j + 1) * D], in_=x_h[nrow0:nrow0 + 128, :]),
                    writes=[f"xpre{j}"], tag=f"ldxp{t + 1}_{j}")
        achunks = None
        if t + 1 < NT:
            gf_a = gated[:].bitcast(F32)
            nblk2 = 2 * D // Tt
            srcs = {C - 2: (xpre[:, 0:D], ["xpre0"]),
                    C - 1: (xpre[:, D:2 * D], ["xpre1"])}
            for j in range(min(2, C - 2)):
                srcs[j] = (gf_a[:, j * D:(j + 1) * D],
                           [f"gated{m}" for m in range(j * nblk2, (j + 1) * nblk2)])
            order = [C - 2, C - 1] + [j for j in range(C - 2) if j in srcs]
            aeng_l = ["act", "act", "act", "dve"]
            achunks = [(c, srcs[c][0], srcs[c][1][0], aeng_l[i],
                        0, "act" if i % 2 == 0 else "dve")
                       for i, c in enumerate(order)]
            aextra = sum((srcs[c][1] for c in order), [])
        for n in range(ND):
            for kg in range(4):
                for mt in range(C):
                    bank = (n * C + mt) % 8

                    def wout_q(e, n=n, mt=mt, bank=bank, kg=kg):
                        last = None
                        for kk in range(MS // 4):
                            k = kg * (MS // 4) + kk
                            last = e.matmul(
                                ps[bank][:, 0:512],
                                gated[:, k * Tt + mt * 128: k * Tt + (mt + 1) * 128],
                                woblk[kg][:, kk * 512:(kk + 1) * 512],
                                start=(k == 0), stop=(k == MS - 1))
                        return last
                    C_.op("pe", wout_q,
                          reads=([f"gated{k}" for k in range(kg * (MS // 4),
                                                            (kg + 1) * (MS // 4))] +
                                 [f"woblk{kg}"]),
                          writes=[f"pb{bank}"], tag=f"wout{t}_{n}_{mt}_{kg}")
            for mt in range(C):
                bank = (n * C + mt) % 8
                C_.op("dve", lambda e, n=n, mt=mt, bank=bank: e.tensor_tensor(
                    X[:, mt * D + n * 512: mt * D + (n + 1) * 512],
                    X[:, mt * D + n * 512: mt * D + (n + 1) * 512],
                    ps[bank][:, 0:512], ALU.add),
                    reads=[f"pb{bank}", f"X{mt}"], writes=[f"X{mt}"],
                    tag=f"xay{t}_{n}_{mt}")
                C_.op("dve", lambda e, n=n, mt=mt: e.bn_stats(
                    stats[:, 6 * (NSUB * mt + n):6 * (NSUB * mt + n) + 6],
                    X[:, mt * D + n * 512: mt * D + (n + 1) * 512]),
                    reads=[f"X{mt}"], writes=[f"stats{mt}"],
                    tag=f"ebns{t}_{mt}_{n}")
            if n == min(1, ND - 1) and achunks is not None:
                ln_stage(t + 1, "a", achunks[0:2], extra_reads=aextra,
                         phase="stats")
                C_.op("act", lambda e: e.activation(sv[0:1, 1:2], cst[0:1, 1:2],
                                                    AF.Exp),
                      reads=["cst"], writes=["svx"], tag=f"tbl{t}")
            if n == ND - 1:
                ech = [(mt, X[:, mt * D:(mt + 1) * D], f"X{mt}",
                        "dve", 0,
                        "dve" if mt % 2 == 0 else "act") for mt in range(C)]
                for mt in range(C):
                    C_.op("dve", lambda e, mt=mt: e.bn_aggr(
                        svc[:, mt * 16:mt * 16 + 2],
                        stats[:, 6 * NSUB * mt:6 * NSUB * (mt + 1)]),
                        reads=[f"stats{mt}"], writes=[f"sv{mt}"],
                        tag=f"ebna{t}_{mt}")
                prev = None
                for mt in range(C):
                    ln_stage(t, "e", [ech[mt]], phase="rest")
                    if prev is not None:
                        bc_evacs(t, prev, 4 * ((prev + 1) % 2))
                    bc_trb(t, mt, 4 * ((mt + 1) % 2))
                    prev = mt
                bc_evacs(t, prev, 4 * ((prev + 1) % 2))
            if n + 1 < ND:
                for kg in range(4):
                    wo_dma(n + 1, kg)

        if t + 1 < NT:
            for kq in range(4):
                wp_dma(0, kq)

        if t + 1 < NT:
            gf = gated[:].bitcast(F32)
            nblk2 = 2 * D // Tt
            for j in range(min(2, C - 2)):
                nrow0 = (t + 1) * Tt + j * 128
                C_.dma("qsp", lambda e, j=j, nrow0=nrow0, gf=gf: e.dma_start(
                    out=gf[:, j * D:(j + 1) * D], in_=x_h[nrow0:nrow0 + 128, :]),
                    writes=[f"gated{m}" for m in range(j * nblk2, (j + 1) * nblk2)],
                    tag=f"ldxg{t + 1}_{j}")

        if t == dbg_t:
            dump("dbg_x1", X[:], [f"X{mt}" for mt in range(C)])

        if t == dbg_t:
            dump("dbg_h2fm", xn_fm[:], [f"xn_fm{k}" for k in range(KD)])

        def sc_emit(e):
            last = None
            for k in range(KD):
                last = e.matmul(ps[0][0:NE, 0:Tt], wr_sb[:, k * NE:(k + 1) * NE],
                                xn_fm[:, k * Tt:(k + 1) * Tt],
                                start=(k == 0), stop=(k == KD - 1))
            return last
        C_.op("pe", sc_emit, reads=["wr_sb"] + [f"xn_fm{k}" for k in range(KD)],
              writes=["pb0"], tag=f"sc{t}")
        C_.op("act", lambda e: e.activation(sc_sb[:], ps[0][0:NE, 0:Tt], AF.Identity),
              reads=["pb0"], writes=["sc_sb"], tag=f"scev{t}")

        def sct_emit(e):
            last = None
            for c in range(C):
                last = e.matmul(ps[1][:, c * NE:(c + 1) * NE],
                                sc_sb[:, c * 128:(c + 1) * 128], id32[0:NE, 0:NE],
                                start=True, stop=True)
            return last
        C_.op("pe", sct_emit, reads=["sc_sb", "id32"], writes=["pb1"], tag=f"sct{t}")
        C_.op("act", lambda e: e.activation(sct[:], ps[1][:, 0:C * NE], AF.Identity),
              reads=["pb1"], writes=["sct"], tag=f"sctev{t}")

        for g in range(G):
            bank = 4 + g

            def hu_emit(e, g=g, bank=bank):
                last = None
                for k in range(KD):
                    last = e.matmul(
                        ps[bank][:, 0:Tt],
                        u_sb[:, k * NE * R + g * 128: k * NE * R + (g + 1) * 128],
                        xn_fm[:, k * Tt:(k + 1) * Tt],
                        start=(k == 0), stop=(k == KD - 1))
                return last
            C_.op("pe", hu_emit, reads=["u_sb"] + [f"xn_fm{k}" for k in range(KD)],
                  writes=[f"pb{bank}"], tag=f"hu{t}_{g}")

        sct3 = sct[:].rearrange("p (c e) -> p c e", e=NE)
        top3 = top8[:].rearrange("p (c j) -> p c j", j=8)
        pexp3 = pexp[:].rearrange("p (c e) -> p c e", e=NE)
        maxv = top3[:, :, 0:1].rearrange("p c j -> p (c j)")
        for c in range(C):
            C_.op("dve", lambda e, c=c: e.max(out=top8[:, c * 8:(c + 1) * 8],
                                              in_=sct[:, c * NE:(c + 1) * NE]),
                  reads=["sct"], writes=["top8"], tag=f"max{t}_{c}")
        C_.op("dve", lambda e: e.memset(top3[:, :, 2:8], -1e30),
              reads=[], writes=["top8"], tag=f"m8{t}")
        for c in range(C):
            C_.op("dve", lambda e, c=c: e.match_replace(
                out=zap[:, c * NE:(c + 1) * NE], in_to_replace=top8[:, c * 8:c * 8 + 8],
                in_values=sct[:, c * NE:(c + 1) * NE], imm_value=-1e30),
                reads=["sct", "top8"], writes=["zap"], tag=f"mr{t}_{c}")
        C_.op("dve", lambda e: e.tensor_tensor(msk[:], zap[:], sct[:], ALU.not_equal),
              reads=["zap", "sct"], writes=["msk"], tag=f"msk{t}")
        C_.op("dve", lambda e: e.tensor_scalar_mul(gv[:, 0:C], maxv, -1.0),
              reads=["top8"], writes=["gv"], tag=f"negm{t}")
        C_.op("dve", lambda e: e.tensor_tensor(
            pexp3, sct3, gv[:, 0:C].rearrange("p (c o) -> p c o", o=1).to_broadcast(
                [128, C, NE]), ALU.add),
            reads=["sct", "gv"], writes=["pexp"], tag=f"sub{t}")
        C_.op("act", lambda e: e.activation(pexp[:], pexp[:], AF.Exp),
              reads=["pexp"], writes=["pexp"], tag=f"exp{t}")
        C_.op("dve", lambda e: e.tensor_reduce(gv[:, 4:4 + C], pexp3, AX.X, ALU.add),
              reads=["pexp"], writes=["gv"], tag=f"den{t}")
        C_.op("dve", lambda e: e.reciprocal(gv[:, 8:8 + C], gv[:, 4:4 + C]),
              reads=["gv"], writes=["gv"], tag=f"rden{t}")
        C_.op("dve", lambda e: e.tensor_tensor(
            pexp3, pexp3, gv[:, 8:8 + C].rearrange("p (c o) -> p c o", o=1).to_broadcast(
                [128, C, NE]), ALU.mult),
            reads=["pexp", "gv"], writes=["pexp"], tag=f"phat{t}")
        C_.op("dve", lambda e: e.tensor_tensor(pexp[:], pexp[:], msk[:], ALU.mult),
              reads=["pexp", "msk"], writes=["pexp"], tag=f"g0{t}")
        C_.op("dve", lambda e: e.tensor_reduce(gv[:, 4:4 + C], pexp3, AX.X, ALU.add),
              reads=["pexp"], writes=["gv"], tag=f"gs{t}")
        C_.op("dve", lambda e: e.tensor_scalar_add(gv[:, 4:4 + C], gv[:, 4:4 + C], 1e-8),
              reads=["gv"], writes=["gv"], tag=f"gse{t}")
        C_.op("dve", lambda e: e.reciprocal(gv[:, 8:8 + C], gv[:, 4:4 + C]),
              reads=["gv"], writes=["gv"], tag=f"rgs{t}")
        C_.op("dve", lambda e: e.tensor_tensor(
            gtok[:].rearrange("p (c e) -> p c e", e=NE), pexp3,
            gv[:, 8:8 + C].rearrange("p (c o) -> p c o", o=1).to_broadcast([128, C, NE]),
            ALU.mult),
            reads=["pexp", "gv"], writes=["gtok"], tag=f"g{t}")

        def a_bn_part(ch, subs, do_aggr):
            (c_, src_, srgn_, _se, _da, _ae) = ch
            for i in subs:
                C_.op("dve", lambda e, i=i, src_=src_, c_=c_: e.bn_stats(
                    stats[:, 6 * (NSUB * c_ + i):6 * (NSUB * c_ + i) + 6],
                    src_[:, 512 * i:512 * (i + 1)]),
                    reads=[srgn_] + aextra, writes=[f"stats{c_}"],
                    tag=f"abns{t + 1}_{c_}_{i}")
            if do_aggr:
                C_.op("dve", lambda e, c_=c_: e.bn_aggr(
                    svca[:, c_ * 16:c_ * 16 + 2],
                    stats[:, 6 * NSUB * c_:6 * NSUB * (c_ + 1)]),
                    reads=[f"stats{c_}"], writes=[f"sva{c_}"],
                    tag=f"abna{t + 1}_{c_}")

        if achunks is not None and len(achunks) > 3:
            a_bn_part(achunks[3], range(0, (NSUB + 1) // 2), False)

        if t == dbg_t:
            dump("dbg_g", gtok[:], ["gtok"])

        def gt_emit(e):
            last = None
            for c in range(C):
                last = e.matmul(ps[2][0:NE, c * 128:(c + 1) * 128],
                                gtok[:, c * NE:(c + 1) * NE], id32[:, :],
                                start=True, stop=True)
            return last
        C_.op("pe", gt_emit, reads=["gtok", "id32"], writes=["pb2"], tag=f"gt{t}")
        C_.op("act", lambda e: e.activation(gfm[:], ps[2][0:NE, 0:Tt], AF.Identity),
              reads=["pb2"], writes=["gfm"], tag=f"gfmev{t}")
        C_.op("pe", lambda e: e.matmul(ps[3][0:1, 0:Tt], aw[:], gfm[:],
                                       start=True, stop=True),
              reads=["aw", "gfm"], writes=["pb3"], tag=f"gate{t}")
        C_.op("act", lambda e: e.activation(sc_sb[0:1, 0:Tt], ps[3][0:1, 0:Tt], AF.Identity,
                                            bias=cst[0:1, 1:2], scale=0.2),
              reads=["pb3", "cst"], writes=["sc_sb"], tag=f"fac{t}")
        C_.op("pe", lambda e: e.matmul(ps[3][0:NE, 0:Tt], ones32[:, 0:NE], sc_sb[0:1, 0:Tt],
                                       start=True, stop=True),
              reads=["ones32", "sc_sb"], writes=["pb3"], tag=f"facx{t}")
        C_.op("dve", lambda e: e.tensor_tensor(g2[:], gfm[:], ps[3][0:NE, 0:Tt], ALU.mult),
              reads=["gfm", "pb3"], writes=["g2"], tag=f"g2{t}")
        if achunks is not None and len(achunks) > 3:
            a_bn_part(achunks[3], range((NSUB + 1) // 2, NSUB), True)
        for g in range(G):
            bank = g % 2
            C_.op("pe", lambda e, g=g, bank=bank: e.matmul(
                ps[bank][:, 0:Tt], ebf[:, g * 128:(g + 1) * 128],
                g2[:], start=True, stop=True),
                reads=["ebf", "g2"], writes=[f"pb{bank}"], tag=f"gexmm{t}_{g}")
            C_.op("act", lambda e, g=g, bank=bank: e.activation(
                gexp[:, g * Tt:(g + 1) * Tt], ps[bank][:, 0:Tt], AF.Identity),
                reads=[f"pb{bank}"], writes=[f"gexp{g}"], tag=f"gexev{t}_{g}")
        for g in range(G):
            bank = 4 + g
            C_.op("dve", lambda e, g=g, bank=bank: e.tensor_tensor(
                gu[:, g * Tt:(g + 1) * Tt], ps[bank][:, 0:Tt],
                gexp[:, g * Tt:(g + 1) * Tt], ALU.mult),
                reads=[f"pb{bank}", f"gexp{g}"], writes=[f"gu{g}"], tag=f"gu{t}_{g}")
        for mt in range(C):
            for n in range(ND):
                bank = 4 + ((mt * ND + n) % 4)

                def v_emit(e, mt=mt, n=n, bank=bank):
                    last = None
                    for k2 in range(G):
                        last = e.matmul(
                            ps[bank][:, 0:512],
                            gu[:, k2 * Tt + mt * 128: k2 * Tt + (mt + 1) * 128],
                            v_sb[:, k2 * D + n * 512: k2 * D + (n + 1) * 512],
                            start=(k2 == 0), stop=(k2 == G - 1))
                    return last
                C_.op("pe", v_emit, reads=[f"gu{k2}" for k2 in range(G)] + ["v_sb"],
                      writes=[f"pb{bank}"], tag=f"vmm{t}_{mt}_{n}")
                C_.op("dve", lambda e, mt=mt, n=n, bank=bank: e.tensor_tensor(
                    X[:, mt * D + n * 512: mt * D + (n + 1) * 512],
                    X[:, mt * D + n * 512: mt * D + (n + 1) * 512],
                    ps[bank][:, 0:512], ALU.add),
                    reads=[f"pb{bank}", f"X{mt}"], writes=[f"X{mt}"],
                    tag=f"xaf{t}_{mt}_{n}")
            row0 = tb + mt * 128
            C_.dma("qsp", lambda e, mt=mt, row0=row0: e.dma_start(
                out=out_h[row0:row0 + 128, :], in_=X[:, mt * D:(mt + 1) * D]),
                reads=[f"X{mt}"], tag=f"st{t}_{mt}")
            if achunks is not None:
                tail_order = [cc for (cc, *_rest) in achunks]
                if mt == 0:
                    ln_stage(t + 1, "a", achunks[0:2], defer_xres=deferred,
                             extra_reads=aextra, phase="rest")
                    ln_stage(t + 1, "a", achunks[2:3], extra_reads=aextra,
                             phase="stats")
                if mt == min(1, C - 2):
                    ln_stage(t + 1, "a", achunks[2:3], defer_xres=deferred,
                             extra_reads=aextra, phase="rest")
                    ln_stage(t + 1, "a", achunks[3:4], defer_xres=deferred,
                             extra_reads=aextra, phase="rest")
                j = mt - 1
                if 0 <= j < len(tail_order) - 1:
                    bc_trb(t + 1, tail_order[j], base=0)
                    bc_evacs(t + 1, tail_order[j], base=0, scale=True)

        if t + 1 < NT:
            last = [cc for (cc, *_rest) in achunks][-1]
            bc_trb(t + 1, last, base=4)
            bc_evacs(t + 1, last, base=4, scale=True)

    C_.barrier_all_dma()

    with es, nc.Block() as block:
        @block.sync
        def _(e):
            C_.emit_stream("sp", e)

        @block.scalar
        def _(e):
            C_.emit_stream("act", e)

        @block.vector
        def _(e):
            C_.emit_stream("dve", e)

        @block.tensor
        def _(e):
            C_.emit_stream("pe", e)

        @block.gpsimd
        def _(e):
            C_.emit_stream("pool", e)

    return nc




def host_layouts(ins, D, S, NE=32, R=16, AD=32):
    MS = S // 128
    KD = D // 128
    KQ = KD // 4
    ND = D // 512
    G = NE * R // 128
    order = ("Wga_w", "Wgb_w", "Wf_w", "Wzg_w", "Wz_w")
    wp6 = np.empty((MS, 4, 128, 5, KQ, 128), dtype=np.float32)
    for pr, name in enumerate(order):
        W = np.asarray(ins[f"L0_attn_{name}"], dtype=np.float32)
        T4 = W.reshape(MS, 128, KD, 128).transpose(0, 3, 2, 1)
        T6 = T4.reshape(MS, 128, 4, KQ, 128).transpose(0, 2, 1, 3, 4)
        wp6[:, :, :, pr] = T6
    wp = np.ascontiguousarray(wp6).reshape(MS * 4 * 128, 5 * KQ * 128)
    WO = np.asarray(ins["L0_attn_Wout_w"], dtype=np.float32)
    wo4 = WO.reshape(ND, 512, MS, 128).transpose(0, 2, 3, 1)
    wo = np.ascontiguousarray(
        wo4.reshape(ND, 4, MS // 4, 128, 512).transpose(0, 1, 3, 2, 4)
    ).reshape(ND * 4 * 128, (MS // 4) * 512)
    U = np.asarray(ins["L0_ffn_U"], dtype=np.float32)
    u = np.ascontiguousarray(
        U.transpose(1, 0, 2).reshape(KD, 128, NE * R).transpose(1, 0, 2)
    ).reshape(128, KD * NE * R)
    V = np.asarray(ins["L0_ffn_V"], dtype=np.float32)
    v = np.ascontiguousarray(
        V.reshape(G, 128, D).transpose(1, 0, 2)).reshape(128, G * D)
    Wr = np.asarray(ins["L0_ffn_Wr"], dtype=np.float32)
    wr = np.ascontiguousarray(
        Wr.T.reshape(KD, 128, NE).transpose(1, 0, 2)).reshape(128, KD * NE)
    rmsw = np.ascontiguousarray(
        np.asarray(ins["L0_attn_rms_w"], dtype=np.float32).reshape(KD, 128).T)
    wfb = np.ascontiguousarray(
        np.asarray(ins["L0_attn_Wf_b"], dtype=np.float32).reshape(MS, 128).T)
    woutb = np.ascontiguousarray(
        np.asarray(ins["L0_attn_Wout_b"], dtype=np.float32).reshape(1, D))
    attrt = np.ascontiguousarray(np.asarray(ins["L0_ffn_attr"], dtype=np.float32).T)
    wat = np.ascontiguousarray(
        np.asarray(ins["L0_ffn_Wa"], dtype=np.float32).reshape(1, AD).T)
    idm = np.eye(128, dtype=np.float32)
    eoh = np.zeros((NE, G * 128), dtype=np.float32)
    for g in range(G):
        for j in range(128):
            eoh[g * (128 // R) + j // R, g * 128 + j] = 1.0
    onesv = np.ones((1, 128), dtype=np.float32)
    return dict(wp=wp, wo=wo, u=u, v=v, wr=wr, rmsw=rmsw, wfb=wfb, woutb=woutb,
                attrt=attrt, wat=wat, idm=idm, eoh=eoh, onesv=onesv)


def kernel(**ins):
    x = np.asarray(ins["L0_attn_x"], dtype=np.float32)
    B, T, D = x.shape
    shared = host_layouts(ins, D=D, S=D)
    nc = build_program(D=D, T=T, S=D, Tt=512)
    in_maps = []
    for b in range(B):
        m = {"x": np.ascontiguousarray(x[b])}
        m.update(shared)
        in_maps.append(m)
    res = run_bass_kernel_spmd(nc, in_maps, core_ids=list(range(B)))
    out = np.stack([np.asarray(res.results[b]["out"]) for b in range(B)], axis=0)
    return out.astype(np.float32)
